# version 11
# baseline (speedup 1.0000x reference)
import math
import numpy as np
import concourse.bass as bass
import concourse.mybir as mybir
from concourse.bass_utils import run_bass_kernel_spmd

F32 = mybir.dt.float32
BF16 = mybir.dt.bfloat16
I32 = mybir.dt.int32
AF = mybir.ActivationFunctionType
ALU = mybir.AluOpType
AX = mybir.AxisListType

NC = 8
D = 1024
HEADS = 16
HD = 64
KT = D // 128
SCALE = float(HD ** (-0.25))
PI = float(np.float32(math.pi))
EPS = 1e-5


def _trunc12(x):
    import struct
    b = struct.unpack('>I', struct.pack('>f', np.float32(x)))[0]
    b &= ~((1 << 11) - 1)
    return float(struct.unpack('>f', struct.pack('>I', b))[0])


TWO_PI = 2.0 * math.pi
CW1 = _trunc12(TWO_PI)
CW2 = _trunc12(TWO_PI - CW1)
CW3 = float(np.float32(TWO_PI - CW1 - CW2))
INV_2PI = float(np.float32(1.0 / TWO_PI))
S_HI = float(np.float32(-math.log(10000.0) / 32.0))
S_LO = float(np.float32(-math.log(10000.0) / 32.0 - S_HI))

NLANES = 4
DMA_LANES = tuple(f"Y{i}" for i in range(NLANES)) + tuple(f"G{i}" for i in range(NLANES))
ENGS = ("T", "V", "S", "G", "Y")
SEM_NAMES = ENGS + DMA_LANES
_AUTO_EDGE = ("V", "S", "G")


class Prog:

    def __init__(self, counts):
        self.q = {e: [] for e in ENGS}
        self.count = counts
        self.entry_waits = [(l, counts[l]) for l in DMA_LANES if counts[l] > 0]

    def op(self, eng, fn, after=(), dma=False):
        waits = {}
        if eng in _AUTO_EDGE and not dma and self.count[eng] > 3:
            waits[eng] = self.count[eng] - 3
        for t in after:
            if t is None:
                continue
            e2, v = t
            if e2 == eng == "T":
                continue
            if v > 0:
                waits[e2] = max(waits.get(e2, 0), v)
        if dma:
            lane_i = self.count[eng + "_rr"] if (eng + "_rr") in self.count else 0
            self.count[eng + "_rr"] = (lane_i + 1) % NLANES
            lane = f"{eng}{lane_i}"
            if self.count[lane] > 0:
                waits[lane] = max(waits.get(lane, 0), self.count[lane])
            self.count[lane] += 16
            tok = (lane, self.count[lane])
            self.q[eng].append((fn, sorted(waits.items()), 16, lane))
            return tok
        self.count[eng] += 1
        tok = (eng, self.count[eng])
        self.q[eng].append((fn, sorted(waits.items()), 1, eng))
        return tok

    def replay(self, eng_name, eng_obj, sems):
        if self.q[eng_name]:
            for e2, v in self.entry_waits:
                eng_obj.wait_ge(sems[e2], v)
        for fn, waits, inc, sem in self.q[eng_name]:
            for e2, v in waits:
                eng_obj.wait_ge(sems[e2], v)
            inst = fn(eng_obj)
            inst.then_inc(sems[sem], inc)


def r3d(ap, p=128):
    return ap.rearrange("(a p) b -> p a b", p=p)


def r3s(ap, a):
    return ap.rearrange("p (a b) -> p a b", a=a)


def build(n_own, n_pre, f_ffn, dbg=False):
    NT = n_own // 128
    NTB = n_own // 512
    NCH = n_own // 128
    NPTB = n_pre // 512
    NPT = n_pre // 128
    NFB = f_ffn // 512
    E = 8

    def stpos(h):
        return 64 * (h % 2), (h // 8) * 260 + ((h % 8) // 2) * 65

    nc = bass.Bass("TRN2", target_bir_lowering=False)

    x_d = nc.dram_tensor("x", [n_own, D], F32, kind="ExternalInput")
    xpre_d = nc.dram_tensor("xpre", [n_pre, D], F32, kind="ExternalInput")
    wqkt_d = nc.dram_tensor("wqkt", [4 * KT, 128, KT * 128], F32, kind="ExternalInput")
    wkvp_d = nc.dram_tensor("wkvp", [4, 128, KT * 512], F32, kind="ExternalInput")
    wout_d = nc.dram_tensor("wout", [128, KT * D], F32, kind="ExternalInput")
    rwt_d = nc.dram_tensor("rwt", [128, KT * 8], F32, kind="ExternalInput")
    bbc_d = nc.dram_tensor("bbc", [128, 8], F32, kind="ExternalInput")
    gw_d = nc.dram_tensor("gw", [E * NFB, 128, KT * 512], F32, kind="ExternalInput")
    uw_d = nc.dram_tensor("uw", [E * NFB, 128, KT * 512], F32, kind="ExternalInput")
    dw_d = nc.dram_tensor("dw", [E * NFB, 128, 4 * D], F32, kind="ExternalInput")
    posb_d = nc.dram_tensor("posb", [128, n_own], F32, kind="ExternalInput")
    posp_d = nc.dram_tensor("posp", [128, NPT], F32, kind="ExternalInput")
    jcol_d = nc.dram_tensor("jcol", [128, 1], F32, kind="ExternalInput")
    j64_d = nc.dram_tensor("j64", [128, 32], F32, kind="ExternalInput")
    sgn_d = nc.dram_tensor("sgn", [128, 1], F32, kind="ExternalInput")
    prem_d = nc.dram_tensor("prem", [128, NPT], F32, kind="ExternalInput")
    ident_d = nc.dram_tensor("ident", [128, 128], F32, kind="ExternalInput")
    cmask_d = nc.dram_tensor("cmask", [128, 128], F32, kind="ExternalInput")
    ccol_d = nc.dram_tensor("ccol", [128, 2], F32, kind="ExternalInput")

    out_d = nc.dram_tensor("out", [n_own, D], F32, kind="ExternalOutput")
    dbg_d = {}
    if dbg:
        for nm, sh in [("d_xn1", [n_own, D]), ("d_qf", [128, KT * n_own]),
                       ("d_kf", [128, KT * n_own]), ("d_vt", [n_own, D]),
                       ("d_st", [128, 8 * 65]), ("d_x1", [n_own, D]),
                       ("d_xn2", [n_own, D]), ("d_w8", [128, NT * 8])]:
            dbg_d[nm] = nc.dram_tensor(nm, sh, F32, kind="ExternalOutput")

    counts = {e: 0 for e in SEM_NAMES}
    counts["Y_rr"] = 0
    counts["G_rr"] = 0
    from contextlib import ExitStack

    with ExitStack() as top:
        sems = {e: top.enter_context(nc.semaphore("sem_" + e)) for e in SEM_NAMES}
        sb = lambda n, s, d=F32: top.enter_context(nc.sbuf_tensor(n, s, d))

        x_res = sb("x_res", [128, NT * D])
        st_f = sb("st_f", [128, 8 * 65])
        st_b = sb("st_b", [128, 8 * 65], BF16)
        invf_c = sb("invf_c", [128, 1])
        invf64 = sb("invf64", [128, 32])
        jcol = sb("jcol_s", [128, 1])
        j64 = sb("j64_s", [128, 32])
        sgn = sb("sgn_s", [128, 1])
        ccol = sb("ccol_s", [128, 2])
        ident_b = sb("ident_b", [128, 128], BF16)
        cmask_b = sb("cmask_b", [128, 128], BF16)
        ones_b = sb("ones_b", [128, 1], BF16)
        posp_s = sb("posp_s", [128, NPT])
        prem_s = sb("prem_s", [128, NPT])
        bbc_s = sb("bbc_s", [128, 8])
        w8_s = sb("w8_s", [128, NT * 8])

        def run_block(p):
            with nc.Block() as blk:
                @blk.tensor
                def _(e):
                    p.replay("T", e, sems)

                @blk.vector
                def _(e):
                    p.replay("V", e, sems)

                @blk.scalar
                def _(e):
                    p.replay("S", e, sems)

                @blk.gpsimd
                def _(e):
                    p.replay("G", e, sems)

                @blk.sync
                def _(e):
                    p.replay("Y", e, sems)

        def emit_ln(p, src_ap_fn, dst_ap_fn, lnc, lcol, src_after=(), dst_after=(), prev=(None, None, None)):
            a0 = p.op("V", lambda e: e.reduce_sum(lcol[:, 0:1], src_ap_fn(), axis=AX.X),
                      after=list(src_after) + [prev[0]])
            a2 = p.op("V", lambda e: e.tensor_tensor(lnc[:, :], src_ap_fn(), src_ap_fn(), ALU.mult),
                      after=list(src_after) + [prev[1]])
            a3 = p.op("V", lambda e: e.reduce_sum(lcol[:, 2:3], lnc[:, :], axis=AX.X),
                      after=[a2, prev[2]])
            a1 = p.op("V", lambda e: e.tensor_scalar(lcol[:, 1:2], lcol[:, 0:1], 1.0 / D, None, ALU.mult), after=[a0])
            a4 = p.op("V", lambda e: e.tensor_tensor(lcol[:, 3:4], lcol[:, 1:2], lcol[:, 1:2], ALU.mult), after=[a1])
            a5 = p.op("V", lambda e: e.tensor_scalar(lcol[:, 2:3], lcol[:, 2:3], 1.0 / D, None, ALU.mult), after=[a3])
            a6 = p.op("V", lambda e: e.tensor_tensor(lcol[:, 2:3], lcol[:, 2:3], lcol[:, 3:4], ALU.subtract), after=[a5, a4])
            a7 = p.op("S", lambda e: e.activation(lcol[:, 2:3], lcol[:, 2:3], AF.Ln, bias=ccol[:, 0:1]), after=[a6])
            a8 = p.op("S", lambda e: e.activation(lcol[:, 3:4], lcol[:, 2:3], AF.Exp, scale=-0.5), after=[a7])
            a9 = p.op("V", lambda e: e.tensor_scalar(dst_ap_fn(), src_ap_fn(), lcol[:, 1:2], lcol[:, 3:4], ALU.subtract, ALU.mult),
                      after=[a8] + list(dst_after))
            return a9, (a1, a3, a7)

        def emit_reduce(p, th, thp, kfb, kib, w, tok_in):
            a = p.op("V", lambda e: e.tensor_scalar(thp[:, :w], th[:, :w], PI, None, ALU.add), after=[tok_in])
            b = p.op("V", lambda e: e.tensor_scalar(kfb[:, :w], thp[:, :w], INV_2PI, None, ALU.mult), after=[a])
            c = p.op("V", lambda e: e.tensor_copy(kib[:, :w], kfb[:, :w]), after=[b])
            d = p.op("V", lambda e: e.tensor_copy(kfb[:, :w], kib[:, :w]), after=[c])
            ng = p.op("V", lambda e: e.tensor_scalar(kfb[:, :w], kfb[:, :w], -1.0, None, ALU.mult), after=[d])
            r = ng
            for cwc in (CW1, CW2, CW3):
                r = p.op("V", lambda e, cc=cwc: e.scalar_tensor_tensor(thp[:, :w], kfb[:, :w], cc, thp[:, :w], ALU.mult, ALU.add), after=[r])
            r = p.op("V", lambda e: e.tensor_scalar(thp[:, :w], thp[:, :w], -PI, None, ALU.add), after=[r])
            g = p.op("V", lambda e: e.tensor_scalar(kfb[:, :w], thp[:, :w], PI, None, ALU.is_gt), after=[r])
            r = p.op("V", lambda e: e.scalar_tensor_tensor(thp[:, :w], kfb[:, :w], -2.0 * PI, thp[:, :w], ALU.mult, ALU.add), after=[g])
            l = p.op("V", lambda e: e.tensor_scalar(kfb[:, :w], thp[:, :w], -PI, None, ALU.is_lt), after=[r])
            r = p.op("V", lambda e: e.scalar_tensor_tensor(thp[:, :w], kfb[:, :w], 2.0 * PI, thp[:, :w], ALU.mult, ALU.add), after=[l])
            r = p.op("V", lambda e: e.tensor_scalar(thp[:, :w], thp[:, :w], PI, -PI, ALU.min, ALU.max), after=[r])
            return r

        p = Prog(counts)
        p.op("Y", lambda e: e.dma_start(
            out=r3s(x_res[:, :], NT), in_=r3d(x_d[:, :])), dma=True)
        for dst, src in [(posp_s, posp_d), (prem_s, prem_d),
                         (jcol, jcol_d), (j64, j64_d), (sgn, sgn_d),
                         (ccol, ccol_d), (bbc_s, bbc_d)]:
            p.op("Y", lambda e, d=dst, s=src: e.dma_start(out=d[:, :], in_=s[:, :]), dma=True)
        p.op("G", lambda e: e.dma_start(out=ident_b[:, :], in_=ident_d[:, :]), dma=True)
        p.op("G", lambda e: e.dma_start(out=cmask_b[:, :], in_=cmask_d[:, :]), dma=True)
        p.op("V", lambda e: e.memset(ones_b[:, :], 1.0))
        p.op("V", lambda e: e.memset(st_f[:, :], 0.0))
        run_block(p)

        p = Prog(counts)
        t0 = p.op("V", lambda e: e.tensor_scalar(invf_c[:, :], jcol[:, :], S_HI, None, ALU.mult))
        t1 = p.op("V", lambda e: e.scalar_tensor_tensor(invf_c[:, :], jcol[:, :], S_LO, invf_c[:, :], ALU.mult, ALU.add), after=[t0])
        p.op("S", lambda e: e.activation(invf_c[:, :], invf_c[:, :], AF.Exp), after=[t1])
        t0 = p.op("V", lambda e: e.tensor_scalar(invf64[:, :], j64[:, :], S_HI, None, ALU.mult))
        t1 = p.op("V", lambda e: e.scalar_tensor_tensor(invf64[:, :], j64[:, :], S_LO, invf64[:, :], ALU.mult, ALU.add), after=[t0])
        p.op("S", lambda e: e.activation(invf64[:, :], invf64[:, :], AF.Exp), after=[t1])
        run_block(p)

        cosmM = sb("cosmM", [128, n_own], BF16)
        sinmM = sb("sinmM", [128, n_own], BF16)
        tabstk = ExitStack()
        coswA = tabstk.enter_context(nc.sbuf_tensor("coswA", [128, NPT * 64], BF16))
        sinwA = tabstk.enter_context(nc.sbuf_tensor("sinwA", [128, NPT * 64], BF16))
        with ExitStack() as ph:
            psb = lambda n, s, d=F32: ph.enter_context(nc.sbuf_tensor(n, s, d))
            WT = NPT * 32 + n_own
            tha = psb("tha", [128, WT])
            thb = psb("thb", [128, WT])
            thk = psb("thk", [128, WT])
            tki = ph.enter_context(nc.sbuf_tensor("tkiT", [128, WT], I32))
            posbf = psb("posbf", [128, n_own])
            p = Prog(counts)
            pb = p.op("Y", lambda e: e.dma_start(out=posbf[:, :], in_=posb_d[:, :]), dma=True)
            ppre = bass.AP(posp_s, 0, [[NPT, 128], [1, NPT], [0, 32]])
            ifre = bass.AP(invf64, 0, [[32, 128], [0, NPT], [1, 32]])
            tA = p.op("V", lambda e: e.tensor_tensor(
                bass.AP(tha, 0, [[WT, 128], [32, NPT], [1, 32]]), ppre, ifre, ALU.mult))
            tB = p.op("V", lambda e: e.tensor_scalar(
                tha[:, NPT * 32: WT], posbf[:, :], invf_c[:, 0:1], None, ALU.mult), after=[pb])
            red = emit_reduce(p, tha, thb, thk, tki, WT, tB)
            sn1 = p.op("S", lambda e: e.activation(tha[:, :], thb[:, :], AF.Sin), after=[red])
            c0 = p.op("V", lambda e: e.tensor_scalar(thk[:, :], thb[:, :], PI / 2.0, None, ALU.add), after=[red])
            c1 = p.op("V", lambda e: e.tensor_scalar(thb[:, :], thk[:, :], PI, None, ALU.is_gt), after=[c0, sn1])
            c2 = p.op("V", lambda e: e.scalar_tensor_tensor(thk[:, :], thb[:, :], -2.0 * PI, thk[:, :], ALU.mult, ALU.add), after=[c1])
            c3 = p.op("V", lambda e: e.tensor_scalar(thk[:, :], thk[:, :], PI, -PI, ALU.min, ALU.max), after=[c2])
            sn2 = p.op("S", lambda e: e.activation(thk[:, :], thk[:, :], AF.Sin), after=[c3])
            f0t = p.op("V", lambda e: e.tensor_scalar(
                bass.AP(sinwA, 0, [[NPT * 64, 128], [64, NPT], [1, 32]]),
                bass.AP(tha, 0, [[WT, 128], [32, NPT], [1, 32]]), -SCALE, None, ALU.mult), after=[sn1])
            f1t = p.op("V", lambda e: e.tensor_scalar(
                bass.AP(sinwA, 32, [[NPT * 64, 128], [64, NPT], [1, 32]]),
                bass.AP(tha, 0, [[WT, 128], [32, NPT], [1, 32]]), SCALE, None, ALU.mult), after=[sn1])
            f2t = p.op("V", lambda e: e.tensor_scalar(
                bass.AP(coswA, 0, [[NPT * 64, 128], [64, NPT], [1, 32]]),
                bass.AP(thk, 0, [[WT, 128], [32, NPT], [1, 32]]), SCALE, None, ALU.mult), after=[sn2])
            f3t = p.op("V", lambda e: e.tensor_scalar(
                bass.AP(coswA, 32, [[NPT * 64, 128], [64, NPT], [1, 32]]),
                bass.AP(thk, 0, [[WT, 128], [32, NPT], [1, 32]]), SCALE, None, ALU.mult), after=[sn2])
            fm0 = p.op("V", lambda e: e.tensor_scalar(sinmM[:, :], tha[:, NPT * 32: WT], sgn[:, 0:1], None, ALU.mult), after=[sn1])
            fm1 = p.op("V", lambda e: e.tensor_scalar(cosmM[:, :], thk[:, NPT * 32: WT], SCALE, None, ALU.mult), after=[sn2])
            run_block(p)

        with ExitStack() as ph:
            psb = lambda n, s, d=F32: ph.enter_context(nc.sbuf_tensor(n, s, d))
            xps = [psb(f"xps{i}", [128, 4 * D], BF16) for i in range(2)]
            xnp_t = psb("xnp_t", [128, 4 * D], BF16)
            xnp_f = psb("xnp_f", [128, KT * 512], BF16)
            kp_t = psb("kp_t", [128, 4 * D], BF16)
            vp_t = psb("vp_t", [128, 4 * D], BF16)
            wk_b = [psb(f"wk_b{i}", [128, KT * 512], BF16) for i in range(2)]
            wv_b = [psb(f"wv_b{i}", [128, KT * 512], BF16) for i in range(2)]
            lnc = [psb(f"lnc{j}", [128, D]) for j in range(2)]
            lcol = [psb(f"lcol{j}", [128, 4]) for j in range(2)]
            rsc1 = [psb(f"rsc1{j}", [128, D], BF16) for j in range(2)]
            rsc2 = [psb(f"rsc2{j}", [128, D], BF16) for j in range(2)]
            psK = [ph.enter_context(nc.psum_tensor(f"psK{i}", [128, 512], F32)) for i in range(2)]
            psV = [ph.enter_context(nc.psum_tensor(f"psV{i}", [128, 512], F32)) for i in range(2)]
            psT = ph.enter_context(nc.psum_tensor("psT", [128, 1024], BF16))
            psS = [ph.enter_context(nc.psum_tensor(f"psS{i}", [128, 512], F32)) for i in range(2)]
            psZ = ph.enter_context(nc.psum_tensor("psZ", [128, 512], F32))

            p = Prog(counts)
            wk_tok, wv_tok = [], []
            for hf in range(2):
                wk_tok.append(p.op("G", lambda e, d=wk_b[hf], hf=hf: e.dma_start(
                    out=d[:, :], in_=wkvp_d[hf, :, :]), dma=True))
                wv_tok.append(p.op("G", lambda e, d=wv_b[hf], hf=hf: e.dma_start(
                    out=d[:, :], in_=wkvp_d[2 + hf, :, :]), dma=True))

            ld_tok = [None, None]
            ln_prev = [(None, None, None), (None, None, None)]
            stage_free = [None, None]
            xnpt_free = None
            xnpf_free = None
            psK_free = [None, None]
            psV_free = [None, None]
            psT_free = None
            kp_free = None
            vp_free = None
            rprev = [[None, None, None], [None, None, None]]
            s_chain = None
            first_s = [True, True]

            for ptb in range(NPTB):
                par = ptb % 2
                ld_tok[par] = p.op("G", lambda e, b=xps[par], i0=ptb * 512:
                                   e.dma_start(out=r3s(b[:, :], 4), in_=r3d(xpre_d[i0:i0 + 512, :])),
                                   after=[stage_free[par]], dma=True)
                ln_done = []
                for i in range(4):
                    tok, ln_prev[i % 2] = emit_ln(
                        p,
                        lambda b=xps[par], i=i: b[:, i * D:(i + 1) * D],
                        lambda i=i: xnp_t[:, i * D:(i + 1) * D],
                        lnc[i % 2], lcol[i % 2],
                        src_after=[ld_tok[par]], dst_after=[xnpt_free], prev=ln_prev[i % 2])
                    ln_done.append(tok)
                stage_free[par] = ln_done[-1]

                tr_last = None
                for i in range(4):
                    for k in range(KT):
                        tm = p.op("T", lambda e, i=i, k=k: e.transpose(
                            psT[:, 0:128], xnp_t[:, i * D + k * 128: i * D + (k + 1) * 128], ident_b[:, :]),
                            after=[ln_done[i], psT_free, xnpf_free])
                        psT_free = p.op("S", lambda e, i=i, k=k: e.copy(
                            xnp_f[:, k * 512 + i * 128: k * 512 + (i + 1) * 128], psT[:, 0:128]), after=[tm])
                        tr_last = psT_free
                xnpt_free = tr_last
                xnpf_ready = tr_last

                kp_ready = {}
                vp_ready = {}
                for i in range(4):
                    for kind in range(2):
                        for hf in range(2):
                            bank = (psK if kind == 0 else psV)[hf]
                            wb = (wk_b if kind == 0 else wv_b)[hf]
                            wt = (wk_tok if kind == 0 else wv_tok)[hf]
                            free = (psK_free if kind == 0 else psV_free)[hf]

                            def mmkv(e, i=i, b=bank, w=wb):
                                last = None
                                for k in range(KT):
                                    last = e.matmul(b[:, :],
                                                    xnp_f[:, k * 512 + i * 128: k * 512 + (i + 1) * 128],
                                                    w[:, k * 512:(k + 1) * 512],
                                                    start=(k == 0), stop=(k == KT - 1))
                                return last
                            mt = p.op("T", mmkv, after=[xnpf_ready, wt, free])
                            dst = kp_t if kind == 0 else vp_t
                            eng_ev = "V" if kind == 0 else "S"
                            ev = p.op(eng_ev, lambda e, d=dst, i=i, hf=hf, b=bank, k0=(kind == 0):
                                      (e.tensor_copy if k0 else e.copy)(
                                      d[:, i * D + hf * 512: i * D + hf * 512 + 512], b[:, :]),
                                      after=[mt, kp_free if kind == 0 else vp_free])
                            if kind == 0:
                                psK_free[hf] = ev
                            else:
                                psV_free[hf] = ev
                            if hf == 1:
                                (kp_ready if kind == 0 else vp_ready)[i] = ev
                xnpf_free = mt

                for i in range(4):
                    ti = ptb * 4 + i
                    pj = i % 2
                    s1b, s2b = rsc1[pj], rsc2[pj]
                    cosb = bass.AP(coswA, ti * 64, [[NPT * 64, 128], [0, 16], [1, 64]])
                    sinb = bass.AP(sinwA, ti * 64, [[NPT * 64, 128], [0, 16], [1, 64]])
                    r0 = p.op("V", lambda e, i=i, cb=cosb, s1b=s1b: e.tensor_tensor(
                        s1b[:, :], kp_t[:, i * D:(i + 1) * D], cb, ALU.mult),
                        after=[kp_ready[i], rprev[pj][0]])
                    r1 = p.op("V", lambda e, i=i, sbp=sinb, s2b=s2b: e.tensor_tensor(
                        s2b[:, :], bass.AP(kp_t, i * D + 32, [[4 * D, 128], [64, 16], [-32, 2], [1, 32]]),
                        sbp, ALU.mult), after=[kp_ready[i], rprev[pj][1], rprev[pj][2]])
                    r2 = p.op("V", lambda e, s1b=s1b, s2b=s2b: e.tensor_tensor(s1b[:, :], s1b[:, :], s2b[:, :], ALU.add), after=[r0, r1])
                    r3 = p.op("V", lambda e, s1b=s1b, s2b=s2b: e.tensor_scalar(s2b[:, :], s1b[:, :], 0.0, None, ALU.min), after=[r2])
                    r4 = p.op("S", lambda e, s2b=s2b: e.activation(s2b[:, :], s2b[:, :], AF.Exp), after=[r3])
                    r5 = p.op("V", lambda e, s1b=s1b: e.tensor_scalar(s1b[:, :], s1b[:, :], 0.0, None, ALU.max), after=[r2, r3])
                    r6 = p.op("V", lambda e, s1b=s1b, s2b=s2b: e.tensor_tensor(s1b[:, :], s1b[:, :], s2b[:, :], ALU.add), after=[r5, r4])
                    r7 = p.op("V", lambda e, i=i, ptb=ptb, s1b=s1b: e.tensor_scalar(
                        kp_t[:, i * D:(i + 1) * D], s1b[:, :],
                        prem_s[:, ptb * 4 + i: ptb * 4 + i + 1], None, ALU.mult), after=[r6, kp_free])
                    kp_ready[i] = r7
                    rprev[pj] = [r7, r6, r4]

                for i in range(4):
                    def smm(e, i=i, first=first_s[0], last_acc=(ptb == NPTB - 1 and i == 3)):
                        last = None
                        for h in range(HEADS):
                            bank = psS[h // 8]
                            rb = 64 * (h % 2)
                            cb = ((h % 8) // 2) * 64
                            st = first and (h % 8 in (0, 1))
                            sp = last_acc and (h % 8 in (6, 7))
                            last = e.matmul(bank[rb:rb + 64, cb:cb + 64],
                                            kp_t[:, i * D + h * HD: i * D + (h + 1) * HD],
                                            vp_t[:, i * D + h * HD: i * D + (h + 1) * HD],
                                            start=st, stop=sp, tile_position=(0, rb),
                                            skip_group_check=True)
                            last = e.matmul(psZ[rb:rb + 64, h:h + 1],
                                            kp_t[:, i * D + h * HD: i * D + (h + 1) * HD],
                                            ones_b[:, :],
                                            start=(first and h in (0, 1)),
                                            stop=(last_acc and h in (HEADS - 2, HEADS - 1)),
                                            tile_position=(0, rb), skip_group_check=True)
                        return last
                    s_chain = p.op("T", smm, after=[kp_ready[i], vp_ready[i], s_chain])
                    first_s = [False, False]
                kp_free = s_chain
                vp_free = s_chain

            evs = []
            for h in range(HEADS):
                rb, scol = stpos(h)
                cb = ((h % 8) // 2) * 64
                evs.append(p.op("V", lambda e, h=h, rb=rb, scol=scol, cb=cb: e.tensor_copy(
                    st_f[rb:rb + 64, scol:scol + 64], psS[h // 8][rb:rb + 64, cb:cb + 64]), after=[s_chain]))
                evs.append(p.op("V", lambda e, h=h, rb=rb, scol=scol: e.tensor_copy(
                    st_f[rb:rb + 64, scol + 64:scol + 65], psZ[rb:rb + 64, h:h + 1]), after=[s_chain]))
            ev2 = p.op("V", lambda e: e.tensor_copy(st_b[:, :], st_f[:, :]), after=evs)
            if dbg:
                p.op("Y", lambda e: e.dma_start(out=dbg_d["d_st"][:, :], in_=st_f[:, :]), after=[ev2], dma=True)
            run_block(p)
        tabstk.close()

        mid = ExitStack()
        qf = mid.enter_context(nc.sbuf_tensor("qf", [128, KT * n_own], BF16))
        kf = mid.enter_context(nc.sbuf_tensor("kf", [128, KT * n_own], BF16))
        vt = mid.enter_context(nc.sbuf_tensor("vt", [128, NT * D], BF16))
        with ExitStack() as ph:
            psb = lambda n, s, d=F32: ph.enter_context(nc.sbuf_tensor(n, s, d))
            xn_t = psb("xn_t", [128, D], BF16)
            xn_f = psb("xn_f", [128, KT * 512], BF16)
            lnc = [psb("lnc2", [128, D])] * 2
            lcol = [psb(f"lcol2{j}", [128, 4]) for j in range(2)]
            ra = [psb(f"ra{j}", [128, 512]) for j in range(2)]
            rb_ = [psb(f"rb{j}", [128, 512]) for j in range(2)]
            wqk = [psb(f"wqk{i}", [128, KT * 128], BF16) for i in range(2)]
            wvb = psb("wvb", [128, KT * 512], BF16)
            psQ = [ph.enter_context(nc.psum_tensor(f"psQ{i}", [128, 512], F32)) for i in range(4)]
            psV2 = [ph.enter_context(nc.psum_tensor(f"psV2{i}", [128, 512], F32)) for i in range(2)]
            psT2 = ph.enter_context(nc.psum_tensor("psT2", [128, 1024], BF16))

            p = Prog(counts)
            xnt_free = []
            dbg_free = []
            ln_prev = [(None, None, None), (None, None, None)]
            xnf_free = None
            psT2_free = None
            qprev = [[None, None], [None, None]]
            wqk_free = [None] * 2
            wvb_free = None
            psQ_free = [None] * 4
            psV2_free = [None] * 2
            last_write = []

            for tb in range(NTB):
                tr_last = None
                for i in range(4):
                    ti = tb * 4 + i
                    tok, ln_prev[i % 2] = emit_ln(
                        p,
                        lambda ti=ti: x_res[:, ti * D:(ti + 1) * D],
                        lambda: xn_t[:, :],
                        lnc[i % 2], lcol[i % 2], dst_after=xnt_free, prev=ln_prev[i % 2])
                    if dbg:
                        dbg_free.append(p.op("G", lambda e, ti=ti: e.dma_start(
                            out=dbg_d["d_xn1"][ti * 128:(ti + 1) * 128, :],
                            in_=xn_t[:, :]), after=[tok], dma=True))
                    for k in range(KT):
                        tm = p.op("T", lambda e, i=i, k=k: e.transpose(
                            psT2[:, 0:128], xn_t[:, k * 128:(k + 1) * 128], ident_b[:, :]),
                            after=[tok, psT2_free, xnf_free])
                        psT2_free = p.op("V", lambda e, i=i, k=k: e.tensor_copy(
                            xn_f[:, k * 512 + i * 128: k * 512 + (i + 1) * 128], psT2[:, 0:128]), after=[tm])
                        tr_last = psT2_free
                    xnt_free = [tr_last] + dbg_free
                    dbg_free = []
                xnf_ready = tr_last

                for pair, dst in ((0, qf), (1, kf)):
                    for ft in range(KT):
                        bi = 0
                        bq, bp = psQ[(ft % 2) * 2], psQ[(ft % 2) * 2 + 1]
                        i_main = pair * 2 * KT + ft
                        i_perm = i_main + KT
                        t_w0 = p.op("G", lambda e, w=wqk[0], ix=i_main: e.dma_start(
                            out=w[:, :], in_=wqkt_d[ix, :, :]),
                            after=[wqk_free[0]], dma=True)
                        t_w1 = p.op("G", lambda e, w=wqk[1], ix=i_perm: e.dma_start(
                            out=w[:, :], in_=wqkt_d[ix, :, :]),
                            after=[wqk_free[1]], dma=True)

                        bqi = (ft % 2) * 2

                        def mmq(e, w=wqk[0], b=bq):
                            last = None
                            for k in range(KT):
                                last = e.matmul(b[:, :], w[:, k * 128:(k + 1) * 128],
                                                xn_f[:, k * 512:(k + 1) * 512],
                                                start=(k == 0), stop=(k == KT - 1))
                            return last
                        m0 = p.op("T", mmq, after=[xnf_ready, t_w0, psQ_free[bqi]])

                        def mmp(e, w=wqk[1], b=bp):
                            last = None
                            for k in range(KT):
                                last = e.matmul(b[:, :], w[:, k * 128:(k + 1) * 128],
                                                xn_f[:, k * 512:(k + 1) * 512],
                                                start=(k == 0), stop=(k == KT - 1))
                            return last
                        m1 = p.op("T", mmp, after=[t_w1, psQ_free[bqi + 1], m0])
                        wqk_free[0] = m0
                        wqk_free[1] = m1
                        pj = ft % 2
                        rab, rbb = ra[pj], rb_[pj]
                        r0 = p.op("V", lambda e, b=bq, tb=tb, rab=rab: e.tensor_tensor(
                            rab[:, :], b[:, :], cosmM[:, tb * 512:(tb + 1) * 512], ALU.mult),
                            after=[m0, qprev[pj][0]])
                        r1 = p.op("V", lambda e, b=bp, tb=tb, rbb=rbb: e.tensor_tensor(
                            rbb[:, :], b[:, :], sinmM[:, tb * 512:(tb + 1) * 512], ALU.mult),
                            after=[m1, qprev[pj][1]])
                        r2 = p.op("V", lambda e, rab=rab, rbb=rbb: e.tensor_tensor(rab[:, :], rab[:, :], rbb[:, :], ALU.add), after=[r0, r1])
                        r3 = p.op("V", lambda e, rab=rab, rbb=rbb: e.tensor_scalar(rbb[:, :], rab[:, :], 0.0, None, ALU.min), after=[r2])
                        r4 = p.op("S", lambda e, rbb=rbb: e.activation(rbb[:, :], rbb[:, :], AF.Exp), after=[r3])
                        r5 = p.op("V", lambda e, rab=rab: e.tensor_scalar(rab[:, :], rab[:, :], 0.0, None, ALU.max), after=[r2, r3])
                        r6 = p.op("V", lambda e, d=dst, ft=ft, tb=tb, rab=rab, rbb=rbb: e.tensor_tensor(
                            d[:, ft * n_own + tb * 512: ft * n_own + (tb + 1) * 512],
                            rab[:, :], rbb[:, :], ALU.add), after=[r5, r4])
                        qprev[pj] = [r6, r4]
                        psQ_free[bqi] = r6
                        psQ_free[bqi + 1] = r6
                        last_write.append(r6)

                for hf in range(2):
                    t_wv = p.op("G", lambda e, hf=hf: e.dma_start(
                        out=wvb[:, :], in_=wkvp_d[2 + hf, :, :]), after=[wvb_free], dma=True)
                    for i in range(4):
                        ti = tb * 4 + i
                        bank = psV2[i % 2]

                        def mmv(e, i=i, b=bank):
                            last = None
                            for k in range(KT):
                                last = e.matmul(b[:, :], xn_f[:, k * 512 + i * 128: k * 512 + (i + 1) * 128],
                                                wvb[:, k * 512:(k + 1) * 512],
                                                start=(k == 0), stop=(k == KT - 1))
                            return last
                        mv = p.op("T", mmv, after=[xnf_ready, t_wv, psV2_free[i % 2]])
                        wvb_free = mv
                        ev = p.op("S", lambda e, ti=ti, hf=hf, b=bank: e.copy(
                            vt[:, ti * D + hf * 512: ti * D + hf * 512 + 512], b[:, :]), after=[mv])
                        psV2_free[i % 2] = ev
                        last_write.append(ev)
                xnf_free = wvb_free

            if dbg:
                lw = last_write[-20:]
                p.op("G", lambda e: e.dma_start(out=dbg_d["d_qf"][:, :], in_=qf[:, :]), after=lw, dma=True)
                p.op("G", lambda e: e.dma_start(out=dbg_d["d_kf"][:, :], in_=kf[:, :]), after=lw, dma=True)
                p.op("G", lambda e: e.dma_start(
                    out=r3d(dbg_d["d_vt"][:, :]), in_=r3s(vt[:, :], NT)), after=lw, dma=True)
            run_block(p)

        with ExitStack() as ph:
            psb = lambda n, s, d=F32: ph.enter_context(nc.sbuf_tensor(n, s, d))
            am = [psb(f"am{i}", [128, 128], BF16) for i in range(2)]
            kcs = [psb(f"kcs{i}", [128, 64], BF16) for i in range(2)]
            dcol = psb("dcol", [128, 2])
            otc = psb("otc", [128, D], BF16)
            otf = psb("otf", [128, KT * 128], BF16)
            woutb = psb("woutb", [128, KT * D], BF16)
            pA = [ph.enter_context(nc.psum_tensor(f"pA{i}", [128, 512], F32)) for i in range(2)]
            pO = [ph.enter_context(nc.psum_tensor(f"pO{i}", [128, 512], F32)) for i in range(2)]
            pS2 = ph.enter_context(nc.psum_tensor("pS2", [128, 512], F32))
            pB16 = ph.enter_context(nc.psum_tensor("pB16", [128, 1024], BF16))
            pP = [ph.enter_context(nc.psum_tensor(f"pP{i}", [128, 512], F32)) for i in range(2)]

            p = Prog(counts)
            t_wo = p.op("G", lambda e: e.dma_start(
                out=woutb[:, :], in_=wout_d[:, :]), dma=True)

            am_free = [None, None]
            kcs_free = [None, None]
            pA_free = [None, None]
            pO_free = [None, None]
            pS2_free = None
            pB16_free = None
            pP_free = [None, None]
            st_rd = {h: None for h in range(HEADS)}
            st_wr = {h: None for h in range(HEADS)}
            otc_wr = [None] * HEADS
            otc_free = None
            otf_free = [None] * KT
            x1_toks = []

            def qsl(buf, h, c):
                return buf[64 * (h % 2):64 * (h % 2) + 64,
                           (h // 2) * n_own + c * 128:(h // 2) * n_own + (c + 1) * 128]

            for c in range(NCH):
                for h in range(HEADS):
                    par = h % 2
                    base, scol = stpos(h)
                    a0 = p.op("T", lambda e, h=h, c=c, par=par: e.matmul(
                        pA[par][:, 0:128], qsl(kf, h, c), qsl(qf, h, c), start=True, stop=True),
                        after=[pA_free[par]])
                    a1 = p.op("V", lambda e, par=par: e.tensor_tensor(
                        am[par][:, :], pA[par][:, 0:128], cmask_b[:, :], ALU.mult), after=[a0, am_free[par]])
                    pA_free[par] = a1
                    k0 = p.op("T", lambda e, h=h, c=c, base=base: e.transpose(
                        pB16[:, 0:64], qsl(kf, h, c), ident_b[base:base + 64, base:base + 64]),
                        after=[pB16_free])
                    k1 = p.op("V", lambda e, par=par: e.tensor_copy(kcs[par][:, :], pB16[:, 0:64]),
                              after=[k0, kcs_free[par]])
                    pB16_free = k1

                    def mmo(e, h=h, c=c, par=par, base=base, scol=scol):
                        ob = pO[par]
                        e.matmul(ob[:, 0:64], am[par][:, :], vt[:, c * D + h * HD: c * D + (h + 1) * HD],
                                 start=True, stop=False)
                        e.matmul(ob[:, 0:64], qsl(qf, h, c), st_b[base:base + 64, scol:scol + 64],
                                 start=False, stop=False)
                        e.matmul(ob[:, 64:65], am[par][:, :], ones_b[:, :], start=False, stop=False)
                        return e.matmul(ob[:, 64:65], qsl(qf, h, c),
                                        st_b[base:base + 64, scol + 64:scol + 65], start=False, stop=True)
                    o0 = p.op("T", mmo, after=[a1, pO_free[par], st_wr[h]])
                    st_rd[h] = o0
                    am_free[par] = o0
                    d0 = p.op("V", lambda e, par=par: e.tensor_scalar(
                        dcol[:, 0:1], pO[par][:, 64:65], 1e-6, None, ALU.max), after=[o0])
                    d1 = p.op("V", lambda e: e.reciprocal(dcol[:, 1:2], dcol[:, 0:1]), after=[d0])
                    d2 = p.op("V", lambda e, h=h, par=par: e.tensor_scalar(
                        otc[:, h * HD:(h + 1) * HD], pO[par][:, 0:64], dcol[:, 1:2], None, ALU.mult),
                        after=[d1, otc_free])
                    pO_free[par] = d2
                    otc_wr[h] = d2

                    def mms(e, h=h, c=c, base=base, par=par):
                        e.matmul(pS2[base:base + 64, 0:64], kcs[par][:, :],
                                 vt[:, c * D + h * HD: c * D + (h + 1) * HD],
                                 start=True, stop=False, tile_position=(0, base))
                        return e.matmul(pS2[base:base + 64, 64:65], kcs[par][:, :], ones_b[:, :],
                                        start=False, stop=True, tile_position=(0, base))
                    s0 = p.op("T", mms, after=[k1, pS2_free])
                    kcs_free[par] = s0
                    s1 = p.op("V", lambda e, base=base, scol=scol: e.tensor_tensor(
                        st_f[base:base + 64, scol:scol + 65], st_f[base:base + 64, scol:scol + 65],
                        pS2[base:base + 64, 0:65], ALU.add), after=[s0, st_rd[h]])
                    s2 = p.op("V", lambda e, base=base, scol=scol: e.tensor_copy(
                        st_b[base:base + 64, scol:scol + 65], st_f[base:base + 64, scol:scol + 65]), after=[s1])
                    pS2_free = s2
                    st_wr[h] = s2

                trs = []
                for k in range(KT):
                    tm = p.op("T", lambda e, k=k: e.transpose(
                        pB16[:, 0:128], otc[:, k * 128:(k + 1) * 128], ident_b[:, :]),
                        after=[pB16_free, otc_wr[2 * k], otc_wr[2 * k + 1], otf_free[k]])
                    pB16_free = p.op("V", lambda e, k=k: e.tensor_copy(
                        otf[:, k * 128:(k + 1) * 128], pB16[:, 0:128]), after=[tm])
                    trs.append(pB16_free)
                otc_free = trs[-1]

                def mmp1(e):
                    last = None
                    for k in range(KT):
                        last = e.matmul(pP[0][:, :], otf[:, k * 128:(k + 1) * 128],
                                        woutb[:, k * D: k * D + 512],
                                        start=(k == 0), stop=(k == KT - 1))
                    return last
                pm0 = p.op("T", mmp1, after=trs + [t_wo, pP_free[0]])

                def mmp2(e):
                    last = None
                    for k in range(KT):
                        last = e.matmul(pP[1][:, :], otf[:, k * 128:(k + 1) * 128],
                                        woutb[:, k * D + 512: (k + 1) * D],
                                        start=(k == 0), stop=(k == KT - 1))
                    return last
                pm1 = p.op("T", mmp2, after=[pm0, pP_free[1]])
                for k in range(KT):
                    otf_free[k] = pm1
                e0 = p.op("V", lambda e, c=c: e.tensor_tensor(
                    x_res[:, c * D: c * D + 512], x_res[:, c * D: c * D + 512], pP[0][:, :], ALU.add), after=[pm0])
                e1 = p.op("V", lambda e, c=c: e.tensor_tensor(
                    x_res[:, c * D + 512: (c + 1) * D], x_res[:, c * D + 512: (c + 1) * D], pP[1][:, :], ALU.add), after=[pm1])
                pP_free[0] = e0
                pP_free[1] = e1
                x1_toks = [e0, e1]

            if dbg:
                p.op("Y", lambda e: e.dma_start(
                    out=r3d(dbg_d["d_x1"][:, :]), in_=r3s(x_res[:, :], NT)), after=x1_toks, dma=True)
            run_block(p)
        mid.close()

        xn2f = sb("xn2f", [128, KT * n_own], BF16)
        with ExitStack() as ph:
            psb = lambda n, s, d=F32: ph.enter_context(nc.sbuf_tensor(n, s, d))
            xn_t = psb("xn_t2", [128, D], BF16)
            lnc = psb("lnc3", [128, D])
            lcol = psb("lcol3", [128, 4])
            rwtb = psb("rwtb", [128, KT * 8], BF16)
            lg = psb("lg", [128, 8])
            l2t = psb("l2t", [128, 8])
            m12 = psb("m12", [128, 6])
            pR = ph.enter_context(nc.psum_tensor("pR", [128, 512], F32))
            pT4 = ph.enter_context(nc.psum_tensor("pT4", [128, 1024], BF16))

            p = Prog(counts)
            t_rw = p.op("G", lambda e: e.dma_start(
                out=rwtb[:, :], in_=rwt_d[:, :]), dma=True)
            pT4_free = None
            pR_free = None
            xnt_free = []
            dbg_free = []
            w8_last = None
            ln_prev = (None, None, None)
            for ti in range(NT):
                a7, ln_prev = emit_ln(
                    p,
                    lambda ti=ti: x_res[:, ti * D:(ti + 1) * D],
                    lambda: xn_t[:, :],
                    lnc, lcol, dst_after=xnt_free, prev=ln_prev)
                if dbg:
                    dbg_free.append(p.op("G", lambda e, ti=ti: e.dma_start(
                        out=dbg_d["d_xn2"][ti * 128:(ti + 1) * 128, :],
                        in_=xn_t[:, :]), after=[a7], dma=True))
                trs = []
                for k in range(KT):
                    tm = p.op("T", lambda e, k=k: e.transpose(
                        pT4[:, 0:128], xn_t[:, k * 128:(k + 1) * 128], ident_b[:, :]),
                        after=[a7, pT4_free])
                    pT4_free = p.op("V", lambda e, k=k, ti=ti: e.tensor_copy(
                        xn2f[:, k * n_own + ti * 128: k * n_own + (ti + 1) * 128], pT4[:, 0:128]), after=[tm])
                    trs.append(pT4_free)
                xnt_free = [trs[-1]] + dbg_free
                dbg_free = []

                def mmr(e, ti=ti):
                    last = None
                    for k in range(KT):
                        last = e.matmul(pR[:, 0:8], xn2f[:, k * n_own + ti * 128: k * n_own + (ti + 1) * 128],
                                        rwtb[:, k * 8:(k + 1) * 8], start=(k == 0), stop=(k == KT - 1))
                    return last
                r0 = p.op("T", mmr, after=trs + [t_rw, pR_free])
                b0 = p.op("V", lambda e: e.tensor_tensor(lg[:, :], pR[:, 0:8], bbc_s[:, :], ALU.add), after=[r0])
                pR_free = b0
                b1 = p.op("V", lambda e: e.reduce_max(m12[:, 0:1], lg[:, :], axis=AX.X), after=[b0])
                b2 = p.op("V", lambda e: e.tensor_scalar(l2t[:, :], lg[:, :], m12[:, 0:1], None, ALU.is_equal), after=[b1])
                b3 = p.op("V", lambda e: e.scalar_tensor_tensor(l2t[:, :], l2t[:, :], -1e30, lg[:, :], ALU.mult, ALU.add), after=[b2])
                b4 = p.op("V", lambda e: e.reduce_max(m12[:, 1:2], l2t[:, :], axis=AX.X), after=[b3])
                b5 = p.op("V", lambda e: e.tensor_tensor(m12[:, 2:3], m12[:, 0:1], m12[:, 1:2], ALU.subtract), after=[b4])
                b6 = p.op("S", lambda e: e.activation(m12[:, 3:4], m12[:, 2:3], AF.Sigmoid), after=[b5])
                b7 = p.op("S", lambda e: e.activation(m12[:, 4:5], m12[:, 2:3], AF.Sigmoid, scale=-1.0), after=[b6])
                b8 = p.op("V", lambda e: e.tensor_scalar(lg[:, :], lg[:, :], m12[:, 0:1], None, ALU.is_equal), after=[b5])
                b9 = p.op("V", lambda e: e.tensor_scalar(l2t[:, :], l2t[:, :], m12[:, 1:2], None, ALU.is_equal), after=[b5])
                ba = p.op("V", lambda e: e.tensor_scalar(lg[:, :], lg[:, :], m12[:, 3:4], None, ALU.mult), after=[b8, b6])
                w8_last = p.op("V", lambda e, ti=ti: e.scalar_tensor_tensor(
                    w8_s[:, ti * 8:(ti + 1) * 8], l2t[:, :], m12[:, 4:5], lg[:, :], ALU.mult, ALU.add),
                    after=[b9, ba, b7])
            if dbg:
                p.op("Y", lambda e: e.dma_start(out=dbg_d["d_w8"][:, :], in_=w8_s[:, :]), after=[w8_last], dma=True)
            run_block(p)

        with ExitStack() as ph:
            psb = lambda n, s, d=F32: ph.enter_context(nc.sbuf_tensor(n, s, d))
            gb = [psb(f"gb{i}", [128, KT * 512], BF16) for i in range(2)]
            ub = [psb(f"ub{i}", [128, KT * 512], BF16) for i in range(2)]
            db_ = [psb(f"db{i}", [128, 4 * D], BF16) for i in range(2)]
            hs = psb("hs", [128, 4 * n_own], BF16)
            sil = [psb(f"sil{i}", [128, 512]) for i in range(2)]
            pG = [ph.enter_context(nc.psum_tensor(f"pG{i}", [128, 512], F32)) for i in range(2)]
            pU = [ph.enter_context(nc.psum_tensor(f"pU{i}", [128, 512], F32)) for i in range(2)]
            pD = [ph.enter_context(nc.psum_tensor(f"pD{i}", [128, 512], F32)) for i in range(2)]

            p = Prog(counts)
            wg_free = [None, None]
            wu_free = [None, None]
            wd_free = [None, None]
            pG_free = [None, None]
            pU_free = [None, None]
            pD_free = [None, None]
            sil_free = [None, None]
            hs_free = None
            out_toks = []

            for ei in range(E):
                for fb in range(NFB):
                    par = (ei * NFB + fb) % 2
                    tg = p.op("G", lambda e, ei=ei, fb=fb, par=par: e.dma_start(
                        out=gb[par][:, :], in_=gw_d[ei * NFB + fb, :, :]),
                        after=[wg_free[par]], dma=True)
                    tu = p.op("G", lambda e, ei=ei, fb=fb, par=par: e.dma_start(
                        out=ub[par][:, :], in_=uw_d[ei * NFB + fb, :, :]),
                        after=[wu_free[par]], dma=True)
                    td = p.op("G", lambda e, ei=ei, fb=fb, par=par: e.dma_start(
                        out=db_[par][:, :], in_=dw_d[ei * NFB + fb, :, :]),
                        after=[wd_free[par]], dma=True)
                    p1_done = []
                    for tb in range(NTB):
                        for f0 in range(4):
                            bpar = f0 % 2

                            def mmg(e, par=par, f0=f0, tb=tb, bpar=bpar):
                                last = None
                                for k in range(KT):
                                    last = e.matmul(pG[bpar][:, :],
                                                    gb[par][:, k * 512 + f0 * 128: k * 512 + (f0 + 1) * 128],
                                                    xn2f[:, k * n_own + tb * 512: k * n_own + (tb + 1) * 512],
                                                    start=(k == 0), stop=(k == KT - 1))
                                return last
                            mg = p.op("T", mmg, after=[tg, pG_free[bpar]])

                            def mmu(e, par=par, f0=f0, tb=tb, bpar=bpar):
                                last = None
                                for k in range(KT):
                                    last = e.matmul(pU[bpar][:, :],
                                                    ub[par][:, k * 512 + f0 * 128: k * 512 + (f0 + 1) * 128],
                                                    xn2f[:, k * n_own + tb * 512: k * n_own + (tb + 1) * 512],
                                                    start=(k == 0), stop=(k == KT - 1))
                                return last
                            mu = p.op("T", mmu, after=[tu, pU_free[bpar], mg])
                            wg_free[par] = mg
                            wu_free[par] = mu
                            sv = p.op("S", lambda e, bpar=bpar: e.activation(
                                sil[bpar][:, :], pG[bpar][:, :], AF.Silu), after=[mg, sil_free[bpar]])
                            pG_free[bpar] = sv
                            hw = p.op("V", lambda e, f0=f0, tb=tb, bpar=bpar: e.tensor_tensor(
                                hs[:, f0 * n_own + tb * 512: f0 * n_own + (tb + 1) * 512],
                                sil[bpar][:, :], pU[bpar][:, :], ALU.mult),
                                after=[sv, mu, hs_free])
                            pU_free[bpar] = hw
                            sil_free[bpar] = hw
                            p1_done.append(hw)
                    last_rd = None
                    for tb in range(NTB):
                        for i in range(4):
                            ti = tb * 4 + i
                            for hf in range(2):
                                bpar = hf

                                def mmd(e, par=par, ti=ti, hf=hf, bpar=bpar):
                                    last = None
                                    for f0 in range(4):
                                        last = e.matmul(pD[bpar][:, :],
                                                        hs[:, f0 * n_own + ti * 128: f0 * n_own + (ti + 1) * 128],
                                                        db_[par][:, f0 * D + hf * 512: f0 * D + (hf + 1) * 512],
                                                        start=(f0 == 0), stop=(f0 == 3))
                                    return last
                                md = p.op("T", mmd, after=p1_done + [td, pD_free[bpar]])
                                wd_free[par] = md
                                last_rd = md
                                ev = p.op("V", lambda e, ti=ti, hf=hf, ei=ei, bpar=bpar: e.scalar_tensor_tensor(
                                    x_res[:, ti * D + hf * 512: ti * D + (hf + 1) * 512],
                                    pD[bpar][:, :], w8_s[:, ti * 8 + ei: ti * 8 + ei + 1],
                                    x_res[:, ti * D + hf * 512: ti * D + (hf + 1) * 512],
                                    ALU.mult, ALU.add), after=[md])
                                pD_free[bpar] = ev
                                out_toks.append(ev)
                    hs_free = last_rd
            run_block(p)

        p = Prog(counts)
        st_tok = p.op("Y", lambda e: e.dma_start(
            out=r3d(out_d[:, :]), in_=r3s(x_res[:, :], NT)),
            after=out_toks[-2 * NT:], dma=True)
        p.op("Y", lambda e: e.nop(), after=[st_tok])
        run_block(p)

    return nc



def _prep_host(L0_attn_x, L0_attn_w_qkv, L0_attn_w_out, L0_ffn_router_w,
               L0_ffn_router_b, L0_ffn_gate_w, L0_ffn_up_w, L0_ffn_down_w,
               n_own, n_pre, f_ffn):
    x = np.ascontiguousarray(np.asarray(L0_attn_x, dtype=np.float32).reshape(-1, D))
    wqkv = np.asarray(L0_attn_w_qkv, dtype=np.float32)
    rw = np.asarray(L0_ffn_router_w, dtype=np.float32)
    rb = np.asarray(L0_ffn_router_b, dtype=np.float32)
    gw = np.asarray(L0_ffn_gate_w, dtype=np.float32)
    uw = np.asarray(L0_ffn_up_w, dtype=np.float32)
    dw = np.asarray(L0_ffn_down_w, dtype=np.float32)

    ntok = x.shape[0]
    assert ntok == NC * n_own

    KTn = D // 128
    NFB = f_ffn // 512

    def pshuf(mat, wtile):
        Wc = mat.shape[1]
        a = mat.reshape(KTn, 128, Wc // wtile, wtile)
        return np.ascontiguousarray(a.transpose(2, 1, 0, 3).reshape(Wc // wtile, 128, KTn * wtile))

    perm = np.arange(D)
    perm = (perm // HD) * HD + ((perm % HD) + HD // 2) % HD
    wq = wqkv[:, 0:D]
    wk = wqkv[:, D:2 * D]
    wv = wqkv[:, 2 * D:3 * D]
    wqkt = np.ascontiguousarray(np.concatenate(
        [pshuf(wq, 128), pshuf(wq[:, perm], 128),
         pshuf(wk, 128), pshuf(wk[:, perm], 128)], axis=0))
    wkvp = np.ascontiguousarray(np.concatenate([pshuf(wk, 512), pshuf(wv, 512)], axis=0))
    wouts = pshuf(np.ascontiguousarray(np.asarray(L0_attn_w_out, dtype=np.float32)), D)[0]
    rwt = pshuf(np.ascontiguousarray(rw.T), 8)[0]
    bbc = np.ascontiguousarray(np.broadcast_to(rb[None, :], (128, 8)).astype(np.float32))
    gwt = np.concatenate([pshuf(np.ascontiguousarray(gw[e].T), 512) for e in range(8)], axis=0)
    uwt = np.concatenate([pshuf(np.ascontiguousarray(uw[e].T), 512) for e in range(8)], axis=0)
    def pshuf_d(mat):
        a = mat.reshape(NFB, 4, 128, D)
        return np.ascontiguousarray(a.transpose(0, 2, 1, 3).reshape(NFB, 128, 4 * D))
    dwt = np.concatenate([pshuf_d(np.ascontiguousarray(dw[e].T)) for e in range(8)], axis=0)

    NPT = n_pre // 128
    r = np.arange(128)
    jcol = (r % 32).astype(np.float32).reshape(128, 1)
    j64 = np.ascontiguousarray(
        np.broadcast_to(np.arange(32, dtype=np.float32)[None, :], (128, 32)))
    sgn = np.where((r % 64) < 32, -np.float32(SCALE), np.float32(SCALE)).astype(np.float32).reshape(128, 1)
    posp = (np.arange(NPT, dtype=np.float32)[None, :] * 128 + r[:, None]).astype(np.float32)
    ident = np.eye(128, dtype=np.float32)
    cmask = np.triu(np.ones((128, 128), dtype=np.float32))
    ccol = np.zeros((128, 2), np.float32)
    ccol[:, 0] = EPS
    ccol[:, 1] = 1.0

    xpre = np.zeros((n_pre, D), dtype=np.float32)
    npre_eff = min(n_pre, (NC - 1) * n_own)
    xpre[:npre_eff] = x[:npre_eff]

    in_maps = []
    for c in range(NC):
        pos0 = c * n_own
        posb = np.ascontiguousarray(np.broadcast_to(
            (pos0 + np.arange(n_own, dtype=np.float32))[None, :], (128, n_own)))
        prem = np.zeros((128, NPT), np.float32)
        npre_c = c * n_own
        full_tiles = npre_c // 128
        prem[:, :full_tiles] = 1.0
        rem = npre_c - full_tiles * 128
        if rem > 0:
            prem[:rem, full_tiles] = 1.0
        in_maps.append(dict(
            x=np.ascontiguousarray(x[c * n_own:(c + 1) * n_own]),
            xpre=xpre, wqkt=wqkt, wkvp=wkvp, wout=wouts, rwt=rwt, bbc=bbc,
            gw=gwt, uw=uwt, dw=dwt,
            posb=posb, posp=posp, jcol=jcol, j64=j64, sgn=sgn,
            prem=prem, ident=ident, cmask=cmask, ccol=ccol,
        ))
    return in_maps


def kernel(L0_attn_x, L0_attn_w_qkv, L0_attn_w_out, L0_ffn_router_w,
           L0_ffn_router_b, L0_ffn_gate_w, L0_ffn_up_w, L0_ffn_down_w):
    n_own = 2048
    n_pre = 14336
    f_ffn = 4096
    in_maps = _prep_host(L0_attn_x, L0_attn_w_qkv, L0_attn_w_out,
                         L0_ffn_router_w, L0_ffn_router_b, L0_ffn_gate_w,
                         L0_ffn_up_w, L0_ffn_down_w, n_own, n_pre, f_ffn)
    nc = build(n_own, n_pre, f_ffn, dbg=False)
    res = run_bass_kernel_spmd(nc, in_maps, core_ids=list(range(NC)))
    out = np.concatenate([np.asarray(r["out"]) for r in res.results], axis=0)
    return out.reshape(1, NC * n_own, D)


# version 12
# speedup vs baseline: 1.0795x; 1.0795x over previous
import math
import numpy as np
import concourse.bass as bass
import concourse.mybir as mybir
from concourse.bass_utils import run_bass_kernel_spmd

F32 = mybir.dt.float32
BF16 = mybir.dt.bfloat16
I32 = mybir.dt.int32
AF = mybir.ActivationFunctionType
ALU = mybir.AluOpType
AX = mybir.AxisListType

NC = 8
D = 1024
HEADS = 16
HD = 64
KT = D // 128
SCALE = float(HD ** (-0.25))
PI = float(np.float32(math.pi))
EPS = 1e-5


def _trunc12(x):
    import struct
    b = struct.unpack('>I', struct.pack('>f', np.float32(x)))[0]
    b &= ~((1 << 11) - 1)
    return float(struct.unpack('>f', struct.pack('>I', b))[0])


TWO_PI = 2.0 * math.pi
CW1 = _trunc12(TWO_PI)
CW2 = _trunc12(TWO_PI - CW1)
CW3 = float(np.float32(TWO_PI - CW1 - CW2))
INV_2PI = float(np.float32(1.0 / TWO_PI))
S_HI = float(np.float32(-math.log(10000.0) / 32.0))
S_LO = float(np.float32(-math.log(10000.0) / 32.0 - S_HI))

NLANES = 4
DMA_LANES = tuple(f"Y{i}" for i in range(NLANES)) + tuple(f"G{i}" for i in range(NLANES))
ENGS = ("T", "V", "S", "G", "Y")
SEM_NAMES = ENGS + DMA_LANES
_AUTO_EDGE = ("V", "S", "G")


class Prog:

    def __init__(self, counts):
        self.q = {e: [] for e in ENGS}
        self.count = counts
        self.entry_waits = [(l, counts[l]) for l in DMA_LANES if counts[l] > 0]

    def op(self, eng, fn, after=(), dma=False):
        waits = {}
        if eng in _AUTO_EDGE and not dma and self.count[eng] > 3:
            waits[eng] = self.count[eng] - 3
        for t in after:
            if t is None:
                continue
            e2, v = t
            if e2 == eng == "T":
                continue
            if v > 0:
                waits[e2] = max(waits.get(e2, 0), v)
        if dma:
            lane_i = self.count[eng + "_rr"] if (eng + "_rr") in self.count else 0
            self.count[eng + "_rr"] = (lane_i + 1) % NLANES
            lane = f"{eng}{lane_i}"
            if self.count[lane] > 0:
                waits[lane] = max(waits.get(lane, 0), self.count[lane])
            self.count[lane] += 16
            tok = (lane, self.count[lane])
            self.q[eng].append((fn, sorted(waits.items()), 16, lane))
            return tok
        self.count[eng] += 1
        tok = (eng, self.count[eng])
        self.q[eng].append((fn, sorted(waits.items()), 1, eng))
        return tok

    def replay(self, eng_name, eng_obj, sems):
        if self.q[eng_name]:
            for e2, v in self.entry_waits:
                eng_obj.wait_ge(sems[e2], v)
        for fn, waits, inc, sem in self.q[eng_name]:
            for e2, v in waits:
                eng_obj.wait_ge(sems[e2], v)
            inst = fn(eng_obj)
            inst.then_inc(sems[sem], inc)


def r3d(ap, p=128):
    return ap.rearrange("(a p) b -> p a b", p=p)


def r3s(ap, a):
    return ap.rearrange("p (a b) -> p a b", a=a)


def build(n_own, n_pre, f_ffn, dbg=False):
    NT = n_own // 128
    NTB = n_own // 512
    NCH = n_own // 128
    NPTB = n_pre // 512
    NPT = n_pre // 128
    NFB = f_ffn // 512
    E = 8

    def stpos(h):
        return 64 * (h % 2), (h // 8) * 260 + ((h % 8) // 2) * 65

    nc = bass.Bass("TRN2", target_bir_lowering=False)

    x_d = nc.dram_tensor("x", [n_own, D], F32, kind="ExternalInput")
    xpre_d = nc.dram_tensor("xpre", [n_pre, D], F32, kind="ExternalInput")
    wqkt_d = nc.dram_tensor("wqkt", [4 * KT, 128, KT * 128], F32, kind="ExternalInput")
    wkvp_d = nc.dram_tensor("wkvp", [4, 128, KT * 512], F32, kind="ExternalInput")
    wout_d = nc.dram_tensor("wout", [128, KT * D], F32, kind="ExternalInput")
    rwt_d = nc.dram_tensor("rwt", [128, KT * 8], F32, kind="ExternalInput")
    bbc_d = nc.dram_tensor("bbc", [128, 8], F32, kind="ExternalInput")
    gw_d = nc.dram_tensor("gw", [E * NFB, 128, KT * 512], F32, kind="ExternalInput")
    uw_d = nc.dram_tensor("uw", [E * NFB, 128, KT * 512], F32, kind="ExternalInput")
    dw_d = nc.dram_tensor("dw", [E * NFB, 128, 4 * D], F32, kind="ExternalInput")
    posb_d = nc.dram_tensor("posb", [128, n_own], F32, kind="ExternalInput")
    posp_d = nc.dram_tensor("posp", [128, NPT], F32, kind="ExternalInput")
    jcol_d = nc.dram_tensor("jcol", [128, 1], F32, kind="ExternalInput")
    j64_d = nc.dram_tensor("j64", [128, 32], F32, kind="ExternalInput")
    sgn_d = nc.dram_tensor("sgn", [128, 1], F32, kind="ExternalInput")
    prem_d = nc.dram_tensor("prem", [128, NPT], F32, kind="ExternalInput")
    ident_d = nc.dram_tensor("ident", [128, 128], F32, kind="ExternalInput")
    cmask_d = nc.dram_tensor("cmask", [128, 128], F32, kind="ExternalInput")
    ccol_d = nc.dram_tensor("ccol", [128, 2], F32, kind="ExternalInput")

    out_d = nc.dram_tensor("out", [n_own, D], F32, kind="ExternalOutput")
    dbg_d = {}
    if dbg:
        for nm, sh in [("d_xn1", [n_own, D]), ("d_qf", [128, KT * n_own]),
                       ("d_kf", [128, KT * n_own]), ("d_vt", [n_own, D]),
                       ("d_st", [128, 8 * 65]), ("d_x1", [n_own, D]),
                       ("d_xn2", [n_own, D]), ("d_w8", [128, NT * 8])]:
            dbg_d[nm] = nc.dram_tensor(nm, sh, F32, kind="ExternalOutput")

    counts = {e: 0 for e in SEM_NAMES}
    counts["Y_rr"] = 0
    counts["G_rr"] = 0
    from contextlib import ExitStack

    with ExitStack() as top:
        sems = {e: top.enter_context(nc.semaphore("sem_" + e)) for e in SEM_NAMES}
        sb = lambda n, s, d=F32: top.enter_context(nc.sbuf_tensor(n, s, d))

        x_res = sb("x_res", [128, NT * D])
        st_f = sb("st_f", [128, 8 * 65])
        st_b = sb("st_b", [128, 8 * 65], BF16)
        invf_c = sb("invf_c", [128, 1])
        invf64 = sb("invf64", [128, 32])
        jcol = sb("jcol_s", [128, 1])
        j64 = sb("j64_s", [128, 32])
        sgn = sb("sgn_s", [128, 1])
        ccol = sb("ccol_s", [128, 2])
        ident_b = sb("ident_b", [128, 128], BF16)
        cmask_b = sb("cmask_b", [128, 128], BF16)
        ones_b = sb("ones_b", [128, 1], BF16)
        posp_s = sb("posp_s", [128, NPT])
        prem_s = sb("prem_s", [128, NPT])
        bbc_s = sb("bbc_s", [128, 8])
        w8_s = sb("w8_s", [128, NT * 8])

        def run_block(p):
            with nc.Block() as blk:
                @blk.tensor
                def _(e):
                    p.replay("T", e, sems)

                @blk.vector
                def _(e):
                    p.replay("V", e, sems)

                @blk.scalar
                def _(e):
                    p.replay("S", e, sems)

                @blk.gpsimd
                def _(e):
                    p.replay("G", e, sems)

                @blk.sync
                def _(e):
                    p.replay("Y", e, sems)

        def emit_ln(p, src_ap_fn, dst_ap_fn, lnc, lcol, src_after=(), dst_after=(), prev=(None, None, None)):
            a0 = p.op("V", lambda e: e.reduce_sum(lcol[:, 0:1], src_ap_fn(), axis=AX.X),
                      after=list(src_after) + [prev[0]])
            a2 = p.op("V", lambda e: e.tensor_tensor(lnc[:, :], src_ap_fn(), src_ap_fn(), ALU.mult),
                      after=list(src_after) + [prev[1]])
            a3 = p.op("V", lambda e: e.reduce_sum(lcol[:, 2:3], lnc[:, :], axis=AX.X),
                      after=[a2, prev[2]])
            a1 = p.op("V", lambda e: e.tensor_scalar(lcol[:, 1:2], lcol[:, 0:1], 1.0 / D, None, ALU.mult), after=[a0])
            a4 = p.op("V", lambda e: e.tensor_tensor(lcol[:, 3:4], lcol[:, 1:2], lcol[:, 1:2], ALU.mult), after=[a1])
            a5 = p.op("V", lambda e: e.tensor_scalar(lcol[:, 2:3], lcol[:, 2:3], 1.0 / D, None, ALU.mult), after=[a3])
            a6 = p.op("V", lambda e: e.tensor_tensor(lcol[:, 2:3], lcol[:, 2:3], lcol[:, 3:4], ALU.subtract), after=[a5, a4])
            a7 = p.op("S", lambda e: e.activation(lcol[:, 2:3], lcol[:, 2:3], AF.Ln, bias=ccol[:, 0:1]), after=[a6])
            a8 = p.op("S", lambda e: e.activation(lcol[:, 3:4], lcol[:, 2:3], AF.Exp, scale=-0.5), after=[a7])
            a9 = p.op("V", lambda e: e.tensor_scalar(dst_ap_fn(), src_ap_fn(), lcol[:, 1:2], lcol[:, 3:4], ALU.subtract, ALU.mult),
                      after=[a8] + list(dst_after))
            return a9, (a1, a3, a7)

        def emit_reduce(p, th, thp, kfb, kib, w, tok_in):
            a = p.op("V", lambda e: e.tensor_scalar(thp[:, :w], th[:, :w], PI, None, ALU.add), after=[tok_in])
            b = p.op("V", lambda e: e.tensor_scalar(kfb[:, :w], thp[:, :w], INV_2PI, None, ALU.mult), after=[a])
            c = p.op("V", lambda e: e.tensor_copy(kib[:, :w], kfb[:, :w]), after=[b])
            d = p.op("V", lambda e: e.tensor_copy(kfb[:, :w], kib[:, :w]), after=[c])
            ng = p.op("V", lambda e: e.tensor_scalar(kfb[:, :w], kfb[:, :w], -1.0, None, ALU.mult), after=[d])
            r = ng
            for cwc in (CW1, CW2, CW3):
                r = p.op("V", lambda e, cc=cwc: e.scalar_tensor_tensor(thp[:, :w], kfb[:, :w], cc, thp[:, :w], ALU.mult, ALU.add), after=[r])
            r = p.op("V", lambda e: e.tensor_scalar(thp[:, :w], thp[:, :w], -PI, None, ALU.add), after=[r])
            g = p.op("V", lambda e: e.tensor_scalar(kfb[:, :w], thp[:, :w], PI, None, ALU.is_gt), after=[r])
            r = p.op("V", lambda e: e.scalar_tensor_tensor(thp[:, :w], kfb[:, :w], -2.0 * PI, thp[:, :w], ALU.mult, ALU.add), after=[g])
            l = p.op("V", lambda e: e.tensor_scalar(kfb[:, :w], thp[:, :w], -PI, None, ALU.is_lt), after=[r])
            r = p.op("V", lambda e: e.scalar_tensor_tensor(thp[:, :w], kfb[:, :w], 2.0 * PI, thp[:, :w], ALU.mult, ALU.add), after=[l])
            r = p.op("V", lambda e: e.tensor_scalar(thp[:, :w], thp[:, :w], PI, -PI, ALU.min, ALU.max), after=[r])
            return r

        p = Prog(counts)
        p.op("Y", lambda e: e.dma_start(
            out=r3s(x_res[:, :], NT), in_=r3d(x_d[:, :])), dma=True)
        for dst, src in [(posp_s, posp_d), (prem_s, prem_d),
                         (jcol, jcol_d), (j64, j64_d), (sgn, sgn_d),
                         (ccol, ccol_d), (bbc_s, bbc_d)]:
            p.op("Y", lambda e, d=dst, s=src: e.dma_start(out=d[:, :], in_=s[:, :]), dma=True)
        p.op("G", lambda e: e.dma_start(out=ident_b[:, :], in_=ident_d[:, :]), dma=True)
        p.op("G", lambda e: e.dma_start(out=cmask_b[:, :], in_=cmask_d[:, :]), dma=True)
        p.op("V", lambda e: e.memset(ones_b[:, :], 1.0))
        p.op("V", lambda e: e.memset(st_f[:, :], 0.0))
        run_block(p)

        p = Prog(counts)
        t0 = p.op("V", lambda e: e.tensor_scalar(invf_c[:, :], jcol[:, :], S_HI, None, ALU.mult))
        t1 = p.op("V", lambda e: e.scalar_tensor_tensor(invf_c[:, :], jcol[:, :], S_LO, invf_c[:, :], ALU.mult, ALU.add), after=[t0])
        p.op("S", lambda e: e.activation(invf_c[:, :], invf_c[:, :], AF.Exp), after=[t1])
        t0 = p.op("V", lambda e: e.tensor_scalar(invf64[:, :], j64[:, :], S_HI, None, ALU.mult))
        t1 = p.op("V", lambda e: e.scalar_tensor_tensor(invf64[:, :], j64[:, :], S_LO, invf64[:, :], ALU.mult, ALU.add), after=[t0])
        p.op("S", lambda e: e.activation(invf64[:, :], invf64[:, :], AF.Exp), after=[t1])
        run_block(p)

        cosmM = sb("cosmM", [128, n_own], BF16)
        sinmM = sb("sinmM", [128, n_own], BF16)
        tabstk = ExitStack()
        coswA = tabstk.enter_context(nc.sbuf_tensor("coswA", [128, NPT * 64], BF16))
        sinwA = tabstk.enter_context(nc.sbuf_tensor("sinwA", [128, NPT * 64], BF16))
        with ExitStack() as ph:
            psb = lambda n, s, d=F32: ph.enter_context(nc.sbuf_tensor(n, s, d))
            WT = NPT * 32 + n_own
            tha = psb("tha", [128, WT])
            thb = psb("thb", [128, WT])
            thk = psb("thk", [128, WT])
            tki = ph.enter_context(nc.sbuf_tensor("tkiT", [128, WT], I32))
            posbf = psb("posbf", [128, n_own])
            p = Prog(counts)
            pb = p.op("Y", lambda e: e.dma_start(out=posbf[:, :], in_=posb_d[:, :]), dma=True)
            ppre = bass.AP(posp_s, 0, [[NPT, 128], [1, NPT], [0, 32]])
            ifre = bass.AP(invf64, 0, [[32, 128], [0, NPT], [1, 32]])
            tA = p.op("V", lambda e: e.tensor_tensor(
                bass.AP(tha, 0, [[WT, 128], [32, NPT], [1, 32]]), ppre, ifre, ALU.mult))
            tB = p.op("V", lambda e: e.tensor_scalar(
                tha[:, NPT * 32: WT], posbf[:, :], invf_c[:, 0:1], None, ALU.mult), after=[pb])
            red = emit_reduce(p, tha, thb, thk, tki, WT, tB)
            sn1 = p.op("S", lambda e: e.activation(tha[:, :], thb[:, :], AF.Sin), after=[red])
            c0 = p.op("V", lambda e: e.tensor_scalar(thk[:, :], thb[:, :], PI / 2.0, None, ALU.add), after=[red])
            c1 = p.op("V", lambda e: e.tensor_scalar(thb[:, :], thk[:, :], PI, None, ALU.is_gt), after=[c0, sn1])
            c2 = p.op("V", lambda e: e.scalar_tensor_tensor(thk[:, :], thb[:, :], -2.0 * PI, thk[:, :], ALU.mult, ALU.add), after=[c1])
            c3 = p.op("V", lambda e: e.tensor_scalar(thk[:, :], thk[:, :], PI, -PI, ALU.min, ALU.max), after=[c2])
            sn2 = p.op("S", lambda e: e.activation(thk[:, :], thk[:, :], AF.Sin), after=[c3])
            f0t = p.op("V", lambda e: e.tensor_scalar(
                bass.AP(sinwA, 0, [[NPT * 64, 128], [64, NPT], [1, 32]]),
                bass.AP(tha, 0, [[WT, 128], [32, NPT], [1, 32]]), -SCALE, None, ALU.mult), after=[sn1])
            f1t = p.op("V", lambda e: e.tensor_scalar(
                bass.AP(sinwA, 32, [[NPT * 64, 128], [64, NPT], [1, 32]]),
                bass.AP(tha, 0, [[WT, 128], [32, NPT], [1, 32]]), SCALE, None, ALU.mult), after=[sn1])
            f2t = p.op("V", lambda e: e.tensor_scalar(
                bass.AP(coswA, 0, [[NPT * 64, 128], [64, NPT], [1, 32]]),
                bass.AP(thk, 0, [[WT, 128], [32, NPT], [1, 32]]), SCALE, None, ALU.mult), after=[sn2])
            f3t = p.op("V", lambda e: e.tensor_scalar(
                bass.AP(coswA, 32, [[NPT * 64, 128], [64, NPT], [1, 32]]),
                bass.AP(thk, 0, [[WT, 128], [32, NPT], [1, 32]]), SCALE, None, ALU.mult), after=[sn2])
            fm0 = p.op("V", lambda e: e.tensor_scalar(sinmM[:, :], tha[:, NPT * 32: WT], sgn[:, 0:1], None, ALU.mult), after=[sn1])
            fm1 = p.op("V", lambda e: e.tensor_scalar(cosmM[:, :], thk[:, NPT * 32: WT], SCALE, None, ALU.mult), after=[sn2])
            run_block(p)

        with ExitStack() as ph:
            psb = lambda n, s, d=F32: ph.enter_context(nc.sbuf_tensor(n, s, d))
            xps = [psb(f"xps{i}", [128, 4 * D], BF16) for i in range(2)]
            xnp_t = psb("xnp_t", [128, 4 * D], BF16)
            xnp_f = psb("xnp_f", [128, KT * 512], BF16)
            kp_t = psb("kp_t", [128, 4 * D], BF16)
            vp_t = psb("vp_t", [128, 4 * D], BF16)
            wk_b = [psb(f"wk_b{i}", [128, KT * 512], BF16) for i in range(2)]
            wv_b = [psb(f"wv_b{i}", [128, KT * 512], BF16) for i in range(2)]
            lnc = [psb(f"lnc{j}", [128, D]) for j in range(2)]
            lcol = [psb(f"lcol{j}", [128, 4]) for j in range(2)]
            rsc1 = [psb(f"rsc1{j}", [128, D], BF16) for j in range(2)]
            rsc2 = [psb(f"rsc2{j}", [128, D], BF16) for j in range(2)]
            psK = [ph.enter_context(nc.psum_tensor(f"psK{i}", [128, 512], F32)) for i in range(2)]
            psV = [ph.enter_context(nc.psum_tensor(f"psV{i}", [128, 512], F32)) for i in range(2)]
            psT = [ph.enter_context(nc.psum_tensor(f"psT{i}", [128, 1024], BF16)) for i in range(2)]
            psS = [ph.enter_context(nc.psum_tensor(f"psS{i}", [128, 512], F32)) for i in range(2)]

            p = Prog(counts)
            wk_tok, wv_tok = [], []
            for hf in range(2):
                wk_tok.append(p.op("G", lambda e, d=wk_b[hf], hf=hf: e.dma_start(
                    out=d[:, :], in_=wkvp_d[hf, :, :]), dma=True))
                wv_tok.append(p.op("G", lambda e, d=wv_b[hf], hf=hf: e.dma_start(
                    out=d[:, :], in_=wkvp_d[2 + hf, :, :]), dma=True))

            ld_tok = [None, None]
            ln_prev = [(None, None, None), (None, None, None)]
            stage_free = [None, None]
            xnpt_free = None
            xnpf_free = None
            psK_free = [None, None]
            psV_free = [None, None]
            psT_free = [None, None]
            kp_free = None
            vp_free = None
            rprev = [[None, None, None], [None, None, None]]
            s_chain = None
            first_s = [True, True]

            for ptb in range(NPTB):
                par = ptb % 2
                ld_tok[par] = p.op("G", lambda e, b=xps[par], i0=ptb * 512:
                                   e.dma_start(out=r3s(b[:, :], 4), in_=r3d(xpre_d[i0:i0 + 512, :])),
                                   after=[stage_free[par]], dma=True)
                ln_done = []
                for i in range(4):
                    tok, ln_prev[i % 2] = emit_ln(
                        p,
                        lambda b=xps[par], i=i: b[:, i * D:(i + 1) * D],
                        lambda i=i: xnp_t[:, i * D:(i + 1) * D],
                        lnc[i % 2], lcol[i % 2],
                        src_after=[ld_tok[par]], dst_after=[xnpt_free], prev=ln_prev[i % 2])
                    ln_done.append(tok)
                stage_free[par] = ln_done[-1]

                tr_last = None
                for i in range(4):
                    for k in range(KT):
                        tp_ = (i * KT + k) % 2
                        tm = p.op("T", lambda e, i=i, k=k, tp_=tp_: e.transpose(
                            psT[tp_][:, 0:128], xnp_t[:, i * D + k * 128: i * D + (k + 1) * 128], ident_b[:, :]),
                            after=[ln_done[i], psT_free[tp_], xnpf_free])
                        psT_free[tp_] = p.op("S", lambda e, i=i, k=k, tp_=tp_: e.copy(
                            xnp_f[:, k * 512 + i * 128: k * 512 + (i + 1) * 128], psT[tp_][:, 0:128]), after=[tm])
                        tr_last = psT_free[tp_]
                xnpt_free = tr_last
                xnpf_ready_l = [psT_free[0], psT_free[1]]
                xnpf_ready = tr_last

                kp_ready = {}
                vp_ready = {}
                for i in range(4):
                    for kind in range(2):
                        for hf in range(2):
                            bank = (psK if kind == 0 else psV)[hf]
                            wb = (wk_b if kind == 0 else wv_b)[hf]
                            wt = (wk_tok if kind == 0 else wv_tok)[hf]
                            free = (psK_free if kind == 0 else psV_free)[hf]

                            def mmkv(e, i=i, b=bank, w=wb):
                                last = None
                                for k in range(KT):
                                    last = e.matmul(b[:, :],
                                                    xnp_f[:, k * 512 + i * 128: k * 512 + (i + 1) * 128],
                                                    w[:, k * 512:(k + 1) * 512],
                                                    start=(k == 0), stop=(k == KT - 1))
                                return last
                            mt = p.op("T", mmkv, after=xnpf_ready_l + [wt, free])
                            dst = kp_t if kind == 0 else vp_t
                            eng_ev = "V" if kind == 0 else "S"
                            ev = p.op(eng_ev, lambda e, d=dst, i=i, hf=hf, b=bank, k0=(kind == 0):
                                      (e.tensor_copy if k0 else e.copy)(
                                      d[:, i * D + hf * 512: i * D + hf * 512 + 512], b[:, :]),
                                      after=[mt, kp_free if kind == 0 else vp_free])
                            if kind == 0:
                                psK_free[hf] = ev
                            else:
                                psV_free[hf] = ev
                            if hf == 1:
                                (kp_ready if kind == 0 else vp_ready)[i] = ev
                xnpf_free = mt

                for i in range(4):
                    ti = ptb * 4 + i
                    pj = i % 2
                    s1b, s2b = rsc1[pj], rsc2[pj]
                    cosb = bass.AP(coswA, ti * 64, [[NPT * 64, 128], [0, 16], [1, 64]])
                    sinb = bass.AP(sinwA, ti * 64, [[NPT * 64, 128], [0, 16], [1, 64]])
                    r0 = p.op("V", lambda e, i=i, cb=cosb, s1b=s1b: e.tensor_tensor(
                        s1b[:, :], kp_t[:, i * D:(i + 1) * D], cb, ALU.mult),
                        after=[kp_ready[i], rprev[pj][0]])
                    r1 = p.op("V", lambda e, i=i, sbp=sinb, s2b=s2b: e.tensor_tensor(
                        s2b[:, :], bass.AP(kp_t, i * D + 32, [[4 * D, 128], [64, 16], [-32, 2], [1, 32]]),
                        sbp, ALU.mult), after=[kp_ready[i], rprev[pj][1], rprev[pj][2]])
                    r2 = p.op("V", lambda e, s1b=s1b, s2b=s2b: e.tensor_tensor(s1b[:, :], s1b[:, :], s2b[:, :], ALU.add), after=[r0, r1])
                    r3 = p.op("V", lambda e, s1b=s1b, s2b=s2b: e.tensor_scalar(s2b[:, :], s1b[:, :], 0.0, None, ALU.min), after=[r2])
                    r4 = p.op("S", lambda e, s2b=s2b: e.activation(s2b[:, :], s2b[:, :], AF.Exp), after=[r3])
                    r5 = p.op("V", lambda e, s1b=s1b: e.tensor_scalar(s1b[:, :], s1b[:, :], 0.0, None, ALU.max), after=[r2, r3])
                    r6 = p.op("V", lambda e, s1b=s1b, s2b=s2b: e.tensor_tensor(s1b[:, :], s1b[:, :], s2b[:, :], ALU.add), after=[r5, r4])
                    r7 = p.op("V", lambda e, i=i, ptb=ptb, s1b=s1b: e.tensor_scalar(
                        kp_t[:, i * D:(i + 1) * D], s1b[:, :],
                        prem_s[:, ptb * 4 + i: ptb * 4 + i + 1], None, ALU.mult), after=[r6, kp_free])
                    kp_ready[i] = r7
                    rprev[pj] = [r7, r6, r4]

                for i in range(4):
                    def smm(e, i=i, first=first_s[0], last_acc=(ptb == NPTB - 1 and i == 3)):
                        last = None
                        for h in range(HEADS):
                            bank = psS[h // 8]
                            rb = 64 * (h % 2)
                            cb = ((h % 8) // 2) * 64
                            st = first and (h % 8 in (0, 1))
                            sp = last_acc and (h % 8 in (6, 7))
                            last = e.matmul(bank[rb:rb + 64, cb:cb + 64],
                                            kp_t[:, i * D + h * HD: i * D + (h + 1) * HD],
                                            vp_t[:, i * D + h * HD: i * D + (h + 1) * HD],
                                            start=st, stop=sp, tile_position=(0, rb),
                                            skip_group_check=True)
                            last = e.matmul(psS[h // 8][rb:rb + 64, 256 + (h % 8): 257 + (h % 8)],
                                            kp_t[:, i * D + h * HD: i * D + (h + 1) * HD],
                                            ones_b[:, :],
                                            start=False, stop=False,
                                            tile_position=(0, rb), skip_group_check=True)
                        return last
                    s_chain = p.op("T", smm, after=[kp_ready[i], vp_ready[i], s_chain])
                    first_s = [False, False]
                kp_free = s_chain
                vp_free = s_chain

            evs = []
            for h in range(HEADS):
                rb, scol = stpos(h)
                cb = ((h % 8) // 2) * 64
                evs.append(p.op("V", lambda e, h=h, rb=rb, scol=scol, cb=cb: e.tensor_copy(
                    st_f[rb:rb + 64, scol:scol + 64], psS[h // 8][rb:rb + 64, cb:cb + 64]), after=[s_chain]))
                evs.append(p.op("V", lambda e, h=h, rb=rb, scol=scol: e.tensor_copy(
                    st_f[rb:rb + 64, scol + 64:scol + 65],
                    psS[h // 8][rb:rb + 64, 256 + (h % 8): 257 + (h % 8)]), after=[s_chain]))
            ev2 = p.op("V", lambda e: e.tensor_copy(st_b[:, :], st_f[:, :]), after=evs)
            if dbg:
                p.op("Y", lambda e: e.dma_start(out=dbg_d["d_st"][:, :], in_=st_f[:, :]), after=[ev2], dma=True)
            run_block(p)
        tabstk.close()

        mid = ExitStack()
        qf = mid.enter_context(nc.sbuf_tensor("qf", [128, KT * n_own], BF16))
        kf = mid.enter_context(nc.sbuf_tensor("kf", [128, KT * n_own], BF16))
        vt = mid.enter_context(nc.sbuf_tensor("vt", [128, NT * D], BF16))
        with ExitStack() as ph:
            psb = lambda n, s, d=F32: ph.enter_context(nc.sbuf_tensor(n, s, d))
            xn_t = psb("xn_t", [128, D], BF16)
            xn_f = psb("xn_f", [128, KT * 512], BF16)
            lnc = [psb("lnc2", [128, D])] * 2
            lcol = [psb(f"lcol2{j}", [128, 4]) for j in range(2)]
            ra = [psb(f"ra{j}", [128, 512]) for j in range(2)]
            rb_ = [psb(f"rb{j}", [128, 512]) for j in range(2)]
            wqk = [psb(f"wqk{i}", [128, KT * 128], BF16) for i in range(2)]
            wvb = psb("wvb", [128, KT * 512], BF16)
            psQ = [ph.enter_context(nc.psum_tensor(f"psQ{i}", [128, 512], F32)) for i in range(4)]
            psV2 = [ph.enter_context(nc.psum_tensor(f"psV2{i}", [128, 512], F32)) for i in range(2)]
            psT2 = [ph.enter_context(nc.psum_tensor(f"psT2{i}", [128, 1024], BF16)) for i in range(2)]

            p = Prog(counts)
            xnt_free = []
            dbg_free = []
            ln_prev = [(None, None, None), (None, None, None)]
            xnf_free = None
            psT2_free = [None, None]
            qprev = [[None, None], [None, None]]
            wqk_free = [None] * 2
            wvb_free = None
            psQ_free = [None] * 4
            psV2_free = [None] * 2
            last_write = []

            for tb in range(NTB):
                tr_last = None
                for i in range(4):
                    ti = tb * 4 + i
                    tok, ln_prev[i % 2] = emit_ln(
                        p,
                        lambda ti=ti: x_res[:, ti * D:(ti + 1) * D],
                        lambda: xn_t[:, :],
                        lnc[i % 2], lcol[i % 2], dst_after=xnt_free, prev=ln_prev[i % 2])
                    if dbg:
                        dbg_free.append(p.op("G", lambda e, ti=ti: e.dma_start(
                            out=dbg_d["d_xn1"][ti * 128:(ti + 1) * 128, :],
                            in_=xn_t[:, :]), after=[tok], dma=True))
                    for k in range(KT):
                        tp_ = k % 2
                        tm = p.op("T", lambda e, i=i, k=k, tp_=tp_: e.transpose(
                            psT2[tp_][:, 0:128], xn_t[:, k * 128:(k + 1) * 128], ident_b[:, :]),
                            after=[tok, psT2_free[tp_], xnf_free])
                        psT2_free[tp_] = p.op("V", lambda e, i=i, k=k, tp_=tp_: e.tensor_copy(
                            xn_f[:, k * 512 + i * 128: k * 512 + (i + 1) * 128], psT2[tp_][:, 0:128]), after=[tm])
                        tr_last = psT2_free[tp_]
                    xnt_free = [tr_last, psT2_free[0], psT2_free[1]] + dbg_free
                    dbg_free = []
                xnf_ready_l = [psT2_free[0], psT2_free[1]]
                xnf_ready = tr_last

                for pair, dst in ((0, qf), (1, kf)):
                    for ft in range(KT):
                        bi = 0
                        bq, bp = psQ[(ft % 2) * 2], psQ[(ft % 2) * 2 + 1]
                        i_main = pair * 2 * KT + ft
                        i_perm = i_main + KT
                        t_w0 = p.op("G", lambda e, w=wqk[0], ix=i_main: e.dma_start(
                            out=w[:, :], in_=wqkt_d[ix, :, :]),
                            after=[wqk_free[0]], dma=True)
                        t_w1 = p.op("G", lambda e, w=wqk[1], ix=i_perm: e.dma_start(
                            out=w[:, :], in_=wqkt_d[ix, :, :]),
                            after=[wqk_free[1]], dma=True)

                        bqi = (ft % 2) * 2

                        def mmq(e, w=wqk[0], b=bq):
                            last = None
                            for k in range(KT):
                                last = e.matmul(b[:, :], w[:, k * 128:(k + 1) * 128],
                                                xn_f[:, k * 512:(k + 1) * 512],
                                                start=(k == 0), stop=(k == KT - 1))
                            return last
                        m0 = p.op("T", mmq, after=xnf_ready_l + [t_w0, psQ_free[bqi]])

                        def mmp(e, w=wqk[1], b=bp):
                            last = None
                            for k in range(KT):
                                last = e.matmul(b[:, :], w[:, k * 128:(k + 1) * 128],
                                                xn_f[:, k * 512:(k + 1) * 512],
                                                start=(k == 0), stop=(k == KT - 1))
                            return last
                        m1 = p.op("T", mmp, after=[t_w1, psQ_free[bqi + 1], m0])
                        wqk_free[0] = m0
                        wqk_free[1] = m1
                        pj = ft % 2
                        rab, rbb = ra[pj], rb_[pj]
                        r0 = p.op("V", lambda e, b=bq, tb=tb, rab=rab: e.tensor_tensor(
                            rab[:, :], b[:, :], cosmM[:, tb * 512:(tb + 1) * 512], ALU.mult),
                            after=[m0, qprev[pj][0]])
                        r1 = p.op("V", lambda e, b=bp, tb=tb, rbb=rbb: e.tensor_tensor(
                            rbb[:, :], b[:, :], sinmM[:, tb * 512:(tb + 1) * 512], ALU.mult),
                            after=[m1, qprev[pj][1]])
                        r2 = p.op("V", lambda e, rab=rab, rbb=rbb: e.tensor_tensor(rab[:, :], rab[:, :], rbb[:, :], ALU.add), after=[r0, r1])
                        r3 = p.op("V", lambda e, rab=rab, rbb=rbb: e.tensor_scalar(rbb[:, :], rab[:, :], 0.0, None, ALU.min), after=[r2])
                        r4 = p.op("S", lambda e, rbb=rbb: e.activation(rbb[:, :], rbb[:, :], AF.Exp), after=[r3])
                        r5 = p.op("V", lambda e, rab=rab: e.tensor_scalar(rab[:, :], rab[:, :], 0.0, None, ALU.max), after=[r2, r3])
                        r6 = p.op("V", lambda e, d=dst, ft=ft, tb=tb, rab=rab, rbb=rbb: e.tensor_tensor(
                            d[:, ft * n_own + tb * 512: ft * n_own + (tb + 1) * 512],
                            rab[:, :], rbb[:, :], ALU.add), after=[r5, r4])
                        qprev[pj] = [r6, r4]
                        psQ_free[bqi] = r6
                        psQ_free[bqi + 1] = r6
                        last_write.append(r6)

                for hf in range(2):
                    t_wv = p.op("G", lambda e, hf=hf: e.dma_start(
                        out=wvb[:, :], in_=wkvp_d[2 + hf, :, :]), after=[wvb_free], dma=True)
                    for i in range(4):
                        ti = tb * 4 + i
                        bank = psV2[i % 2]

                        def mmv(e, i=i, b=bank):
                            last = None
                            for k in range(KT):
                                last = e.matmul(b[:, :], xn_f[:, k * 512 + i * 128: k * 512 + (i + 1) * 128],
                                                wvb[:, k * 512:(k + 1) * 512],
                                                start=(k == 0), stop=(k == KT - 1))
                            return last
                        mv = p.op("T", mmv, after=xnf_ready_l + [t_wv, psV2_free[i % 2]])
                        wvb_free = mv
                        ev = p.op("S", lambda e, ti=ti, hf=hf, b=bank: e.copy(
                            vt[:, ti * D + hf * 512: ti * D + hf * 512 + 512], b[:, :]), after=[mv])
                        psV2_free[i % 2] = ev
                        last_write.append(ev)
                xnf_free = wvb_free

            if dbg:
                lw = last_write[-20:]
                p.op("G", lambda e: e.dma_start(out=dbg_d["d_qf"][:, :], in_=qf[:, :]), after=lw, dma=True)
                p.op("G", lambda e: e.dma_start(out=dbg_d["d_kf"][:, :], in_=kf[:, :]), after=lw, dma=True)
                p.op("G", lambda e: e.dma_start(
                    out=r3d(dbg_d["d_vt"][:, :]), in_=r3s(vt[:, :], NT)), after=lw, dma=True)
            run_block(p)

        with ExitStack() as ph:
            psb = lambda n, s, d=F32: ph.enter_context(nc.sbuf_tensor(n, s, d))
            am = [psb(f"am{i}", [128, 128], BF16) for i in range(2)]
            kcs = [psb(f"kcs{i}", [128, 64], BF16) for i in range(2)]
            dcol = psb("dcol", [128, 2])
            otc = psb("otc", [128, D], BF16)
            otf = psb("otf", [128, KT * 128], BF16)
            woutb = psb("woutb", [128, KT * D], BF16)
            pA = [ph.enter_context(nc.psum_tensor(f"pA{i}", [128, 512], F32)) for i in range(2)]
            pO = [ph.enter_context(nc.psum_tensor(f"pO{i}", [128, 512], F32)) for i in range(2)]
            pS2 = ph.enter_context(nc.psum_tensor("pS2", [128, 512], F32))
            pB16 = ph.enter_context(nc.psum_tensor("pB16", [128, 1024], BF16))
            pP = [ph.enter_context(nc.psum_tensor(f"pP{i}", [128, 512], F32)) for i in range(2)]

            p = Prog(counts)
            t_wo = p.op("G", lambda e: e.dma_start(
                out=woutb[:, :], in_=wout_d[:, :]), dma=True)

            am_free = [None, None]
            kcs_free = [None, None]
            pA_free = [None, None]
            pO_free = [None, None]
            pS2_free = None
            pB16_free = None
            pP_free = [None, None]
            st_rd = {h: None for h in range(HEADS)}
            st_wr = {h: None for h in range(HEADS)}
            otc_wr = [None] * HEADS
            otc_free = None
            otf_free = [None] * KT
            x1_toks = []

            def qsl(buf, h, c):
                return buf[64 * (h % 2):64 * (h % 2) + 64,
                           (h // 2) * n_own + c * 128:(h // 2) * n_own + (c + 1) * 128]

            for c in range(NCH):
                for h in range(HEADS):
                    par = h % 2
                    base, scol = stpos(h)
                    a0 = p.op("T", lambda e, h=h, c=c, par=par: e.matmul(
                        pA[par][:, 0:128], qsl(kf, h, c), qsl(qf, h, c), start=True, stop=True),
                        after=[pA_free[par]])
                    a1 = p.op("V", lambda e, par=par: e.tensor_tensor(
                        am[par][:, :], pA[par][:, 0:128], cmask_b[:, :], ALU.mult), after=[a0, am_free[par]])
                    pA_free[par] = a1
                    k0 = p.op("T", lambda e, h=h, c=c, base=base: e.transpose(
                        pB16[:, 0:64], qsl(kf, h, c), ident_b[base:base + 64, base:base + 64]),
                        after=[pB16_free])
                    k1 = p.op("V", lambda e, par=par: e.tensor_copy(kcs[par][:, :], pB16[:, 0:64]),
                              after=[k0, kcs_free[par]])
                    pB16_free = k1

                    def mmo(e, h=h, c=c, par=par, base=base, scol=scol):
                        ob = pO[par]
                        e.matmul(ob[:, 0:64], am[par][:, :], vt[:, c * D + h * HD: c * D + (h + 1) * HD],
                                 start=True, stop=False)
                        e.matmul(ob[:, 0:64], qsl(qf, h, c), st_b[base:base + 64, scol:scol + 64],
                                 start=False, stop=False)
                        e.matmul(ob[:, 64:65], am[par][:, :], ones_b[:, :], start=False, stop=False)
                        return e.matmul(ob[:, 64:65], qsl(qf, h, c),
                                        st_b[base:base + 64, scol + 64:scol + 65], start=False, stop=True)
                    o0 = p.op("T", mmo, after=[a1, pO_free[par], st_wr[h]])
                    st_rd[h] = o0
                    am_free[par] = o0
                    d0 = p.op("V", lambda e, par=par: e.tensor_scalar(
                        dcol[:, 0:1], pO[par][:, 64:65], 1e-6, None, ALU.max), after=[o0])
                    d1 = p.op("V", lambda e: e.reciprocal(dcol[:, 1:2], dcol[:, 0:1]), after=[d0])
                    d2 = p.op("V", lambda e, h=h, par=par: e.tensor_scalar(
                        otc[:, h * HD:(h + 1) * HD], pO[par][:, 0:64], dcol[:, 1:2], None, ALU.mult),
                        after=[d1, otc_free])
                    pO_free[par] = d2
                    otc_wr[h] = d2

                    def mms(e, h=h, c=c, base=base, par=par):
                        e.matmul(pS2[base:base + 64, 0:64], kcs[par][:, :],
                                 vt[:, c * D + h * HD: c * D + (h + 1) * HD],
                                 start=True, stop=False, tile_position=(0, base))
                        return e.matmul(pS2[base:base + 64, 64:65], kcs[par][:, :], ones_b[:, :],
                                        start=False, stop=True, tile_position=(0, base))
                    s0 = p.op("T", mms, after=[k1, pS2_free])
                    kcs_free[par] = s0
                    s1 = p.op("V", lambda e, base=base, scol=scol: e.tensor_tensor(
                        st_f[base:base + 64, scol:scol + 65], st_f[base:base + 64, scol:scol + 65],
                        pS2[base:base + 64, 0:65], ALU.add), after=[s0, st_rd[h]])
                    s2 = p.op("V", lambda e, base=base, scol=scol: e.tensor_copy(
                        st_b[base:base + 64, scol:scol + 65], st_f[base:base + 64, scol:scol + 65]), after=[s1])
                    pS2_free = s2
                    st_wr[h] = s2

                trs = []
                for k in range(KT):
                    tm = p.op("T", lambda e, k=k: e.transpose(
                        pB16[:, 0:128], otc[:, k * 128:(k + 1) * 128], ident_b[:, :]),
                        after=[pB16_free, otc_wr[2 * k], otc_wr[2 * k + 1], otf_free[k]])
                    pB16_free = p.op("V", lambda e, k=k: e.tensor_copy(
                        otf[:, k * 128:(k + 1) * 128], pB16[:, 0:128]), after=[tm])
                    trs.append(pB16_free)
                otc_free = trs[-1]

                def mmp1(e):
                    last = None
                    for k in range(KT):
                        last = e.matmul(pP[0][:, :], otf[:, k * 128:(k + 1) * 128],
                                        woutb[:, k * D: k * D + 512],
                                        start=(k == 0), stop=(k == KT - 1))
                    return last
                pm0 = p.op("T", mmp1, after=trs + [t_wo, pP_free[0]])

                def mmp2(e):
                    last = None
                    for k in range(KT):
                        last = e.matmul(pP[1][:, :], otf[:, k * 128:(k + 1) * 128],
                                        woutb[:, k * D + 512: (k + 1) * D],
                                        start=(k == 0), stop=(k == KT - 1))
                    return last
                pm1 = p.op("T", mmp2, after=[pm0, pP_free[1]])
                for k in range(KT):
                    otf_free[k] = pm1
                e0 = p.op("V", lambda e, c=c: e.tensor_tensor(
                    x_res[:, c * D: c * D + 512], x_res[:, c * D: c * D + 512], pP[0][:, :], ALU.add), after=[pm0])
                e1 = p.op("V", lambda e, c=c: e.tensor_tensor(
                    x_res[:, c * D + 512: (c + 1) * D], x_res[:, c * D + 512: (c + 1) * D], pP[1][:, :], ALU.add), after=[pm1])
                pP_free[0] = e0
                pP_free[1] = e1
                x1_toks = [e0, e1]

            if dbg:
                p.op("Y", lambda e: e.dma_start(
                    out=r3d(dbg_d["d_x1"][:, :]), in_=r3s(x_res[:, :], NT)), after=x1_toks, dma=True)
            run_block(p)
        mid.close()

        xn2f = sb("xn2f", [128, KT * n_own], BF16)
        with ExitStack() as ph:
            psb = lambda n, s, d=F32: ph.enter_context(nc.sbuf_tensor(n, s, d))
            xn_t = psb("xn_t2", [128, D], BF16)
            lnc = psb("lnc3", [128, D])
            lcol = psb("lcol3", [128, 4])
            rwtb = psb("rwtb", [128, KT * 8], BF16)
            lg = psb("lg", [128, 8])
            l2t = psb("l2t", [128, 8])
            m12 = psb("m12", [128, 6])
            pR = ph.enter_context(nc.psum_tensor("pR", [128, 512], F32))
            pT4 = [ph.enter_context(nc.psum_tensor(f"pT4{i}", [128, 1024], BF16)) for i in range(2)]

            p = Prog(counts)
            t_rw = p.op("G", lambda e: e.dma_start(
                out=rwtb[:, :], in_=rwt_d[:, :]), dma=True)
            pT4_free = [None, None]
            pR_free = None
            xnt_free = []
            dbg_free = []
            w8_last = None
            ln_prev = (None, None, None)
            for ti in range(NT):
                a7, ln_prev = emit_ln(
                    p,
                    lambda ti=ti: x_res[:, ti * D:(ti + 1) * D],
                    lambda: xn_t[:, :],
                    lnc, lcol, dst_after=xnt_free, prev=ln_prev)
                if dbg:
                    dbg_free.append(p.op("G", lambda e, ti=ti: e.dma_start(
                        out=dbg_d["d_xn2"][ti * 128:(ti + 1) * 128, :],
                        in_=xn_t[:, :]), after=[a7], dma=True))
                trs = []
                for k in range(KT):
                    tp_ = k % 2
                    tm = p.op("T", lambda e, k=k, tp_=tp_: e.transpose(
                        pT4[tp_][:, 0:128], xn_t[:, k * 128:(k + 1) * 128], ident_b[:, :]),
                        after=[a7, pT4_free[tp_]])
                    pT4_free[tp_] = p.op("V", lambda e, k=k, ti=ti, tp_=tp_: e.tensor_copy(
                        xn2f[:, k * n_own + ti * 128: k * n_own + (ti + 1) * 128], pT4[tp_][:, 0:128]), after=[tm])
                    trs.append(pT4_free[tp_])
                xnt_free = [trs[-1], trs[-2]] + dbg_free
                dbg_free = []

                def mmr(e, ti=ti):
                    last = None
                    for k in range(KT):
                        last = e.matmul(pR[:, 0:8], xn2f[:, k * n_own + ti * 128: k * n_own + (ti + 1) * 128],
                                        rwtb[:, k * 8:(k + 1) * 8], start=(k == 0), stop=(k == KT - 1))
                    return last
                r0 = p.op("T", mmr, after=trs + [t_rw, pR_free])
                b0 = p.op("V", lambda e: e.tensor_tensor(lg[:, :], pR[:, 0:8], bbc_s[:, :], ALU.add), after=[r0])
                pR_free = b0
                b1 = p.op("V", lambda e: e.reduce_max(m12[:, 0:1], lg[:, :], axis=AX.X), after=[b0])
                b2 = p.op("V", lambda e: e.tensor_scalar(l2t[:, :], lg[:, :], m12[:, 0:1], None, ALU.is_equal), after=[b1])
                b3 = p.op("V", lambda e: e.scalar_tensor_tensor(l2t[:, :], l2t[:, :], -1e30, lg[:, :], ALU.mult, ALU.add), after=[b2])
                b4 = p.op("V", lambda e: e.reduce_max(m12[:, 1:2], l2t[:, :], axis=AX.X), after=[b3])
                b5 = p.op("V", lambda e: e.tensor_tensor(m12[:, 2:3], m12[:, 0:1], m12[:, 1:2], ALU.subtract), after=[b4])
                b6 = p.op("S", lambda e: e.activation(m12[:, 3:4], m12[:, 2:3], AF.Sigmoid), after=[b5])
                b7 = p.op("S", lambda e: e.activation(m12[:, 4:5], m12[:, 2:3], AF.Sigmoid, scale=-1.0), after=[b6])
                b8 = p.op("V", lambda e: e.tensor_scalar(lg[:, :], lg[:, :], m12[:, 0:1], None, ALU.is_equal), after=[b5])
                b9 = p.op("V", lambda e: e.tensor_scalar(l2t[:, :], l2t[:, :], m12[:, 1:2], None, ALU.is_equal), after=[b5])
                ba = p.op("V", lambda e: e.tensor_scalar(lg[:, :], lg[:, :], m12[:, 3:4], None, ALU.mult), after=[b8, b6])
                w8_last = p.op("V", lambda e, ti=ti: e.scalar_tensor_tensor(
                    w8_s[:, ti * 8:(ti + 1) * 8], l2t[:, :], m12[:, 4:5], lg[:, :], ALU.mult, ALU.add),
                    after=[b9, ba, b7])
            if dbg:
                p.op("Y", lambda e: e.dma_start(out=dbg_d["d_w8"][:, :], in_=w8_s[:, :]), after=[w8_last], dma=True)
            run_block(p)

        with ExitStack() as ph:
            psb = lambda n, s, d=F32: ph.enter_context(nc.sbuf_tensor(n, s, d))
            gb = [psb(f"gb{i}", [128, KT * 512], BF16) for i in range(2)]
            ub = [psb(f"ub{i}", [128, KT * 512], BF16) for i in range(2)]
            db_ = [psb(f"db{i}", [128, 4 * D], BF16) for i in range(2)]
            hs = psb("hs", [128, 4 * n_own], BF16)
            sil = [psb(f"sil{i}", [128, 512]) for i in range(2)]
            pG = [ph.enter_context(nc.psum_tensor(f"pG{i}", [128, 512], F32)) for i in range(2)]
            pU = [ph.enter_context(nc.psum_tensor(f"pU{i}", [128, 512], F32)) for i in range(2)]
            pD = [ph.enter_context(nc.psum_tensor(f"pD{i}", [128, 512], F32)) for i in range(2)]

            p = Prog(counts)
            wg_free = [None, None]
            wu_free = [None, None]
            wd_free = [None, None]
            pG_free = [None, None]
            pU_free = [None, None]
            pD_free = [None, None]
            sil_free = [None, None]
            hs_free = None
            out_toks = []

            for ei in range(E):
                for fb in range(NFB):
                    par = (ei * NFB + fb) % 2
                    tg = p.op("G", lambda e, ei=ei, fb=fb, par=par: e.dma_start(
                        out=gb[par][:, :], in_=gw_d[ei * NFB + fb, :, :]),
                        after=[wg_free[par]], dma=True)
                    tu = p.op("G", lambda e, ei=ei, fb=fb, par=par: e.dma_start(
                        out=ub[par][:, :], in_=uw_d[ei * NFB + fb, :, :]),
                        after=[wu_free[par]], dma=True)
                    td = p.op("G", lambda e, ei=ei, fb=fb, par=par: e.dma_start(
                        out=db_[par][:, :], in_=dw_d[ei * NFB + fb, :, :]),
                        after=[wd_free[par]], dma=True)
                    p1_done = []
                    for tb in range(NTB):
                        for f0 in range(4):
                            bpar = f0 % 2

                            def mmg(e, par=par, f0=f0, tb=tb, bpar=bpar):
                                last = None
                                for k in range(KT):
                                    last = e.matmul(pG[bpar][:, :],
                                                    gb[par][:, k * 512 + f0 * 128: k * 512 + (f0 + 1) * 128],
                                                    xn2f[:, k * n_own + tb * 512: k * n_own + (tb + 1) * 512],
                                                    start=(k == 0), stop=(k == KT - 1))
                                return last
                            mg = p.op("T", mmg, after=[tg, pG_free[bpar]])

                            def mmu(e, par=par, f0=f0, tb=tb, bpar=bpar):
                                last = None
                                for k in range(KT):
                                    last = e.matmul(pU[bpar][:, :],
                                                    ub[par][:, k * 512 + f0 * 128: k * 512 + (f0 + 1) * 128],
                                                    xn2f[:, k * n_own + tb * 512: k * n_own + (tb + 1) * 512],
                                                    start=(k == 0), stop=(k == KT - 1))
                                return last
                            mu = p.op("T", mmu, after=[tu, pU_free[bpar], mg])
                            wg_free[par] = mg
                            wu_free[par] = mu
                            sv = p.op("S", lambda e, bpar=bpar: e.activation(
                                sil[bpar][:, :], pG[bpar][:, :], AF.Silu), after=[mg, sil_free[bpar]])
                            pG_free[bpar] = sv
                            hw = p.op("V", lambda e, f0=f0, tb=tb, bpar=bpar: e.tensor_tensor(
                                hs[:, f0 * n_own + tb * 512: f0 * n_own + (tb + 1) * 512],
                                sil[bpar][:, :], pU[bpar][:, :], ALU.mult),
                                after=[sv, mu, hs_free])
                            pU_free[bpar] = hw
                            sil_free[bpar] = hw
                            p1_done.append(hw)
                    last_rd = None
                    for tb in range(NTB):
                        for i in range(4):
                            ti = tb * 4 + i
                            for hf in range(2):
                                bpar = hf

                                def mmd(e, par=par, ti=ti, hf=hf, bpar=bpar):
                                    last = None
                                    for f0 in range(4):
                                        last = e.matmul(pD[bpar][:, :],
                                                        hs[:, f0 * n_own + ti * 128: f0 * n_own + (ti + 1) * 128],
                                                        db_[par][:, f0 * D + hf * 512: f0 * D + (hf + 1) * 512],
                                                        start=(f0 == 0), stop=(f0 == 3))
                                    return last
                                md = p.op("T", mmd, after=p1_done + [td, pD_free[bpar]])
                                wd_free[par] = md
                                last_rd = md
                                ev = p.op("V", lambda e, ti=ti, hf=hf, ei=ei, bpar=bpar: e.scalar_tensor_tensor(
                                    x_res[:, ti * D + hf * 512: ti * D + (hf + 1) * 512],
                                    pD[bpar][:, :], w8_s[:, ti * 8 + ei: ti * 8 + ei + 1],
                                    x_res[:, ti * D + hf * 512: ti * D + (hf + 1) * 512],
                                    ALU.mult, ALU.add), after=[md])
                                pD_free[bpar] = ev
                                out_toks.append(ev)
                    hs_free = last_rd
            run_block(p)

        p = Prog(counts)
        st_tok = p.op("Y", lambda e: e.dma_start(
            out=r3d(out_d[:, :]), in_=r3s(x_res[:, :], NT)),
            after=out_toks[-2 * NT:], dma=True)
        p.op("Y", lambda e: e.nop(), after=[st_tok])
        run_block(p)

    return nc



def _prep_host(L0_attn_x, L0_attn_w_qkv, L0_attn_w_out, L0_ffn_router_w,
               L0_ffn_router_b, L0_ffn_gate_w, L0_ffn_up_w, L0_ffn_down_w,
               n_own, n_pre, f_ffn):
    x = np.ascontiguousarray(np.asarray(L0_attn_x, dtype=np.float32).reshape(-1, D))
    wqkv = np.asarray(L0_attn_w_qkv, dtype=np.float32)
    rw = np.asarray(L0_ffn_router_w, dtype=np.float32)
    rb = np.asarray(L0_ffn_router_b, dtype=np.float32)
    gw = np.asarray(L0_ffn_gate_w, dtype=np.float32)
    uw = np.asarray(L0_ffn_up_w, dtype=np.float32)
    dw = np.asarray(L0_ffn_down_w, dtype=np.float32)

    ntok = x.shape[0]
    assert ntok == NC * n_own

    KTn = D // 128
    NFB = f_ffn // 512

    def pshuf(mat, wtile):
        Wc = mat.shape[1]
        a = mat.reshape(KTn, 128, Wc // wtile, wtile)
        return np.ascontiguousarray(a.transpose(2, 1, 0, 3).reshape(Wc // wtile, 128, KTn * wtile))

    perm = np.arange(D)
    perm = (perm // HD) * HD + ((perm % HD) + HD // 2) % HD
    wq = wqkv[:, 0:D]
    wk = wqkv[:, D:2 * D]
    wv = wqkv[:, 2 * D:3 * D]
    wqkt = np.ascontiguousarray(np.concatenate(
        [pshuf(wq, 128), pshuf(wq[:, perm], 128),
         pshuf(wk, 128), pshuf(wk[:, perm], 128)], axis=0))
    wkvp = np.ascontiguousarray(np.concatenate([pshuf(wk, 512), pshuf(wv, 512)], axis=0))
    wouts = pshuf(np.ascontiguousarray(np.asarray(L0_attn_w_out, dtype=np.float32)), D)[0]
    rwt = pshuf(np.ascontiguousarray(rw.T), 8)[0]
    bbc = np.ascontiguousarray(np.broadcast_to(rb[None, :], (128, 8)).astype(np.float32))
    gwt = np.concatenate([pshuf(np.ascontiguousarray(gw[e].T), 512) for e in range(8)], axis=0)
    uwt = np.concatenate([pshuf(np.ascontiguousarray(uw[e].T), 512) for e in range(8)], axis=0)
    def pshuf_d(mat):
        a = mat.reshape(NFB, 4, 128, D)
        return np.ascontiguousarray(a.transpose(0, 2, 1, 3).reshape(NFB, 128, 4 * D))
    dwt = np.concatenate([pshuf_d(np.ascontiguousarray(dw[e].T)) for e in range(8)], axis=0)

    NPT = n_pre // 128
    r = np.arange(128)
    jcol = (r % 32).astype(np.float32).reshape(128, 1)
    j64 = np.ascontiguousarray(
        np.broadcast_to(np.arange(32, dtype=np.float32)[None, :], (128, 32)))
    sgn = np.where((r % 64) < 32, -np.float32(SCALE), np.float32(SCALE)).astype(np.float32).reshape(128, 1)
    posp = (np.arange(NPT, dtype=np.float32)[None, :] * 128 + r[:, None]).astype(np.float32)
    ident = np.eye(128, dtype=np.float32)
    cmask = np.triu(np.ones((128, 128), dtype=np.float32))
    ccol = np.zeros((128, 2), np.float32)
    ccol[:, 0] = EPS
    ccol[:, 1] = 1.0

    xpre = np.zeros((n_pre, D), dtype=np.float32)
    npre_eff = min(n_pre, (NC - 1) * n_own)
    xpre[:npre_eff] = x[:npre_eff]

    in_maps = []
    for c in range(NC):
        pos0 = c * n_own
        posb = np.ascontiguousarray(np.broadcast_to(
            (pos0 + np.arange(n_own, dtype=np.float32))[None, :], (128, n_own)))
        prem = np.zeros((128, NPT), np.float32)
        npre_c = c * n_own
        full_tiles = npre_c // 128
        prem[:, :full_tiles] = 1.0
        rem = npre_c - full_tiles * 128
        if rem > 0:
            prem[:rem, full_tiles] = 1.0
        in_maps.append(dict(
            x=np.ascontiguousarray(x[c * n_own:(c + 1) * n_own]),
            xpre=xpre, wqkt=wqkt, wkvp=wkvp, wout=wouts, rwt=rwt, bbc=bbc,
            gw=gwt, uw=uwt, dw=dwt,
            posb=posb, posp=posp, jcol=jcol, j64=j64, sgn=sgn,
            prem=prem, ident=ident, cmask=cmask, ccol=ccol,
        ))
    return in_maps


def kernel(L0_attn_x, L0_attn_w_qkv, L0_attn_w_out, L0_ffn_router_w,
           L0_ffn_router_b, L0_ffn_gate_w, L0_ffn_up_w, L0_ffn_down_w):
    n_own = 2048
    n_pre = 14336
    f_ffn = 4096
    in_maps = _prep_host(L0_attn_x, L0_attn_w_qkv, L0_attn_w_out,
                         L0_ffn_router_w, L0_ffn_router_b, L0_ffn_gate_w,
                         L0_ffn_up_w, L0_ffn_down_w, n_own, n_pre, f_ffn)
    nc = build(n_own, n_pre, f_ffn, dbg=False)
    res = run_bass_kernel_spmd(nc, in_maps, core_ids=list(range(NC)))
    out = np.concatenate([np.asarray(r["out"]) for r in res.results], axis=0)
    return out.reshape(1, NC * n_own, D)


# version 13
# speedup vs baseline: 1.1018x; 1.0207x over previous
import math
import numpy as np
import concourse.bass as bass
import concourse.mybir as mybir
from concourse.bass_utils import run_bass_kernel_spmd

F32 = mybir.dt.float32
BF16 = mybir.dt.bfloat16
I32 = mybir.dt.int32
AF = mybir.ActivationFunctionType
ALU = mybir.AluOpType
AX = mybir.AxisListType

NC = 8
D = 1024
HEADS = 16
HD = 64
KT = D // 128
SCALE = float(HD ** (-0.25))
PI = float(np.float32(math.pi))
EPS = 1e-5


def _trunc12(x):
    import struct
    b = struct.unpack('>I', struct.pack('>f', np.float32(x)))[0]
    b &= ~((1 << 11) - 1)
    return float(struct.unpack('>f', struct.pack('>I', b))[0])


TWO_PI = 2.0 * math.pi
CW1 = _trunc12(TWO_PI)
CW2 = _trunc12(TWO_PI - CW1)
CW3 = float(np.float32(TWO_PI - CW1 - CW2))
INV_2PI = float(np.float32(1.0 / TWO_PI))
S_HI = float(np.float32(-math.log(10000.0) / 32.0))
S_LO = float(np.float32(-math.log(10000.0) / 32.0 - S_HI))

NLANES = 4
DMA_LANES = tuple(f"Y{i}" for i in range(NLANES)) + tuple(f"G{i}" for i in range(NLANES))
ENGS = ("T", "V", "S", "G", "Y")
SEM_NAMES = ENGS + DMA_LANES
_AUTO_EDGE = ("V", "S", "G")


class Prog:

    def __init__(self, counts):
        self.q = {e: [] for e in ENGS}
        self.count = counts
        self.entry_waits = [(l, counts[l]) for l in DMA_LANES if counts[l] > 0]

    def op(self, eng, fn, after=(), dma=False):
        waits = {}
        if eng in _AUTO_EDGE and not dma and self.count[eng] > 3:
            waits[eng] = self.count[eng] - 3
        for t in after:
            if t is None:
                continue
            e2, v = t
            if e2 == eng == "T":
                continue
            if v > 0:
                waits[e2] = max(waits.get(e2, 0), v)
        if dma:
            lane_i = self.count[eng + "_rr"] if (eng + "_rr") in self.count else 0
            self.count[eng + "_rr"] = (lane_i + 1) % NLANES
            lane = f"{eng}{lane_i}"
            if self.count[lane] > 0:
                waits[lane] = max(waits.get(lane, 0), self.count[lane])
            self.count[lane] += 16
            tok = (lane, self.count[lane])
            self.q[eng].append((fn, sorted(waits.items()), 16, lane))
            return tok
        self.count[eng] += 1
        tok = (eng, self.count[eng])
        self.q[eng].append((fn, sorted(waits.items()), 1, eng))
        return tok

    def replay(self, eng_name, eng_obj, sems):
        if self.q[eng_name]:
            for e2, v in self.entry_waits:
                eng_obj.wait_ge(sems[e2], v)
        for fn, waits, inc, sem in self.q[eng_name]:
            for e2, v in waits:
                eng_obj.wait_ge(sems[e2], v)
            inst = fn(eng_obj)
            inst.then_inc(sems[sem], inc)


def r3d(ap, p=128):
    return ap.rearrange("(a p) b -> p a b", p=p)


def r3s(ap, a):
    return ap.rearrange("p (a b) -> p a b", a=a)


def build(n_own, n_pre, f_ffn, dbg=False):
    NT = n_own // 128
    NTB = n_own // 512
    NCH = n_own // 128
    NPTB = n_pre // 512
    NPT = n_pre // 128
    NFB = f_ffn // 512
    E = 8

    def stpos(h):
        return 64 * (h % 2), (h // 8) * 260 + ((h % 8) // 2) * 65

    nc = bass.Bass("TRN2", target_bir_lowering=False)

    x_d = nc.dram_tensor("x", [n_own, D], F32, kind="ExternalInput")
    xpre_d = nc.dram_tensor("xpre", [n_pre, D], F32, kind="ExternalInput")
    wqkt_d = nc.dram_tensor("wqkt", [4 * KT, 128, KT * 128], F32, kind="ExternalInput")
    wkvp_d = nc.dram_tensor("wkvp", [4, 128, KT * 512], F32, kind="ExternalInput")
    wout_d = nc.dram_tensor("wout", [128, KT * D], F32, kind="ExternalInput")
    rwt_d = nc.dram_tensor("rwt", [128, KT * 8], F32, kind="ExternalInput")
    bbc_d = nc.dram_tensor("bbc", [128, 8], F32, kind="ExternalInput")
    gw_d = nc.dram_tensor("gw", [E * NFB, 128, KT * 512], F32, kind="ExternalInput")
    uw_d = nc.dram_tensor("uw", [E * NFB, 128, KT * 512], F32, kind="ExternalInput")
    dw_d = nc.dram_tensor("dw", [E * NFB, 128, 4 * D], F32, kind="ExternalInput")
    posb_d = nc.dram_tensor("posb", [128, n_own], F32, kind="ExternalInput")
    posp_d = nc.dram_tensor("posp", [128, NPT], F32, kind="ExternalInput")
    jcol_d = nc.dram_tensor("jcol", [128, 1], F32, kind="ExternalInput")
    j64_d = nc.dram_tensor("j64", [128, 32], F32, kind="ExternalInput")
    sgn_d = nc.dram_tensor("sgn", [128, 1], F32, kind="ExternalInput")
    prem_d = nc.dram_tensor("prem", [128, NPT], F32, kind="ExternalInput")
    ident_d = nc.dram_tensor("ident", [128, 128], F32, kind="ExternalInput")
    cmask_d = nc.dram_tensor("cmask", [128, 128], F32, kind="ExternalInput")
    ccol_d = nc.dram_tensor("ccol", [128, 2], F32, kind="ExternalInput")

    out_d = nc.dram_tensor("out", [n_own, D], F32, kind="ExternalOutput")
    dbg_d = {}
    if dbg:
        for nm, sh in [("d_xn1", [n_own, D]), ("d_qf", [128, KT * n_own]),
                       ("d_kf", [128, KT * n_own]), ("d_vt", [n_own, D]),
                       ("d_st", [128, 8 * 65]), ("d_x1", [n_own, D]),
                       ("d_xn2", [n_own, D]), ("d_w8", [128, NT * 8])]:
            dbg_d[nm] = nc.dram_tensor(nm, sh, F32, kind="ExternalOutput")

    counts = {e: 0 for e in SEM_NAMES}
    counts["Y_rr"] = 0
    counts["G_rr"] = 0
    from contextlib import ExitStack

    with ExitStack() as top:
        sems = {e: top.enter_context(nc.semaphore("sem_" + e)) for e in SEM_NAMES}
        sb = lambda n, s, d=F32: top.enter_context(nc.sbuf_tensor(n, s, d))

        x_res = sb("x_res", [128, NT * D])
        st_f = sb("st_f", [128, 8 * 65])
        st_b = sb("st_b", [128, 8 * 65], BF16)
        invf_c = sb("invf_c", [128, 1])
        invf64 = sb("invf64", [128, 32])
        jcol = sb("jcol_s", [128, 1])
        j64 = sb("j64_s", [128, 32])
        sgn = sb("sgn_s", [128, 1])
        ccol = sb("ccol_s", [128, 2])
        ident_b = sb("ident_b", [128, 128], BF16)
        cmask_b = sb("cmask_b", [128, 128], BF16)
        ones_b = sb("ones_b", [128, 1], BF16)
        posp_s = sb("posp_s", [128, NPT])
        prem_s = sb("prem_s", [128, NPT])
        bbc_s = sb("bbc_s", [128, 8])
        w8_s = sb("w8_s", [128, NT * 8])

        def run_block(p):
            with nc.Block() as blk:
                @blk.tensor
                def _(e):
                    p.replay("T", e, sems)

                @blk.vector
                def _(e):
                    p.replay("V", e, sems)

                @blk.scalar
                def _(e):
                    p.replay("S", e, sems)

                @blk.gpsimd
                def _(e):
                    p.replay("G", e, sems)

                @blk.sync
                def _(e):
                    p.replay("Y", e, sems)

        def emit_ln(p, src_ap_fn, dst_ap_fn, lnc, lcol, src_after=(), dst_after=(), prev=(None, None, None)):
            a0 = p.op("V", lambda e: e.reduce_sum(lcol[:, 0:1], src_ap_fn(), axis=AX.X),
                      after=list(src_after) + [prev[0]])
            a2 = p.op("V", lambda e: e.tensor_tensor(lnc[:, :], src_ap_fn(), src_ap_fn(), ALU.mult),
                      after=list(src_after) + [prev[1]])
            a3 = p.op("V", lambda e: e.reduce_sum(lcol[:, 2:3], lnc[:, :], axis=AX.X),
                      after=[a2, prev[2]])
            a1 = p.op("V", lambda e: e.tensor_scalar(lcol[:, 1:2], lcol[:, 0:1], 1.0 / D, None, ALU.mult), after=[a0])
            a4 = p.op("V", lambda e: e.tensor_tensor(lcol[:, 3:4], lcol[:, 1:2], lcol[:, 1:2], ALU.mult), after=[a1])
            a5 = p.op("V", lambda e: e.tensor_scalar(lcol[:, 2:3], lcol[:, 2:3], 1.0 / D, None, ALU.mult), after=[a3])
            a6 = p.op("V", lambda e: e.tensor_tensor(lcol[:, 2:3], lcol[:, 2:3], lcol[:, 3:4], ALU.subtract), after=[a5, a4])
            a7 = p.op("S", lambda e: e.activation(lcol[:, 2:3], lcol[:, 2:3], AF.Ln, bias=ccol[:, 0:1]), after=[a6])
            a8 = p.op("S", lambda e: e.activation(lcol[:, 3:4], lcol[:, 2:3], AF.Exp, scale=-0.5), after=[a7])
            a9 = p.op("V", lambda e: e.tensor_scalar(dst_ap_fn(), src_ap_fn(), lcol[:, 1:2], lcol[:, 3:4], ALU.subtract, ALU.mult),
                      after=[a8] + list(dst_after))
            return a9, (a1, a3, a7)

        def emit_reduce(p, th, thp, kfb, kib, w, tok_in):
            a = p.op("V", lambda e: e.tensor_scalar(thp[:, :w], th[:, :w], PI, None, ALU.add), after=[tok_in])
            b = p.op("V", lambda e: e.tensor_scalar(kfb[:, :w], thp[:, :w], INV_2PI, None, ALU.mult), after=[a])
            c = p.op("V", lambda e: e.tensor_copy(kib[:, :w], kfb[:, :w]), after=[b])
            d = p.op("V", lambda e: e.tensor_copy(kfb[:, :w], kib[:, :w]), after=[c])
            ng = p.op("V", lambda e: e.tensor_scalar(kfb[:, :w], kfb[:, :w], -1.0, None, ALU.mult), after=[d])
            r = ng
            for cwc in (CW1, CW2, CW3):
                r = p.op("V", lambda e, cc=cwc: e.scalar_tensor_tensor(thp[:, :w], kfb[:, :w], cc, thp[:, :w], ALU.mult, ALU.add), after=[r])
            r = p.op("V", lambda e: e.tensor_scalar(thp[:, :w], thp[:, :w], -PI, None, ALU.add), after=[r])
            g = p.op("V", lambda e: e.tensor_scalar(kfb[:, :w], thp[:, :w], PI, None, ALU.is_gt), after=[r])
            r = p.op("V", lambda e: e.scalar_tensor_tensor(thp[:, :w], kfb[:, :w], -2.0 * PI, thp[:, :w], ALU.mult, ALU.add), after=[g])
            l = p.op("V", lambda e: e.tensor_scalar(kfb[:, :w], thp[:, :w], -PI, None, ALU.is_lt), after=[r])
            r = p.op("V", lambda e: e.scalar_tensor_tensor(thp[:, :w], kfb[:, :w], 2.0 * PI, thp[:, :w], ALU.mult, ALU.add), after=[l])
            r = p.op("V", lambda e: e.tensor_scalar(thp[:, :w], thp[:, :w], PI, -PI, ALU.min, ALU.max), after=[r])
            return r

        p = Prog(counts)
        p.op("Y", lambda e: e.dma_start(
            out=r3s(x_res[:, :], NT), in_=r3d(x_d[:, :])), dma=True)
        for dst, src in [(posp_s, posp_d), (prem_s, prem_d),
                         (jcol, jcol_d), (j64, j64_d), (sgn, sgn_d),
                         (ccol, ccol_d), (bbc_s, bbc_d)]:
            p.op("Y", lambda e, d=dst, s=src: e.dma_start(out=d[:, :], in_=s[:, :]), dma=True)
        p.op("G", lambda e: e.dma_start(out=ident_b[:, :], in_=ident_d[:, :]), dma=True)
        p.op("G", lambda e: e.dma_start(out=cmask_b[:, :], in_=cmask_d[:, :]), dma=True)
        p.op("V", lambda e: e.memset(ones_b[:, :], 1.0))
        p.op("V", lambda e: e.memset(st_f[:, :], 0.0))
        run_block(p)

        p = Prog(counts)
        t0 = p.op("V", lambda e: e.tensor_scalar(invf_c[:, :], jcol[:, :], S_HI, None, ALU.mult))
        t1 = p.op("V", lambda e: e.scalar_tensor_tensor(invf_c[:, :], jcol[:, :], S_LO, invf_c[:, :], ALU.mult, ALU.add), after=[t0])
        p.op("S", lambda e: e.activation(invf_c[:, :], invf_c[:, :], AF.Exp), after=[t1])
        t0 = p.op("V", lambda e: e.tensor_scalar(invf64[:, :], j64[:, :], S_HI, None, ALU.mult))
        t1 = p.op("V", lambda e: e.scalar_tensor_tensor(invf64[:, :], j64[:, :], S_LO, invf64[:, :], ALU.mult, ALU.add), after=[t0])
        p.op("S", lambda e: e.activation(invf64[:, :], invf64[:, :], AF.Exp), after=[t1])
        run_block(p)

        cosmM = sb("cosmM", [128, n_own], BF16)
        sinmM = sb("sinmM", [128, n_own], BF16)
        tabstk = ExitStack()
        coswA = tabstk.enter_context(nc.sbuf_tensor("coswA", [128, NPT * 64], BF16))
        sinwA = tabstk.enter_context(nc.sbuf_tensor("sinwA", [128, NPT * 64], BF16))
        with ExitStack() as ph:
            psb = lambda n, s, d=F32: ph.enter_context(nc.sbuf_tensor(n, s, d))
            WT = NPT * 32 + n_own
            tha = psb("tha", [128, WT])
            thb = psb("thb", [128, WT])
            thk = psb("thk", [128, WT])
            tki = ph.enter_context(nc.sbuf_tensor("tkiT", [128, WT], I32))
            posbf = psb("posbf", [128, n_own])
            p = Prog(counts)
            pb = p.op("Y", lambda e: e.dma_start(out=posbf[:, :], in_=posb_d[:, :]), dma=True)
            ppre = bass.AP(posp_s, 0, [[NPT, 128], [1, NPT], [0, 32]])
            ifre = bass.AP(invf64, 0, [[32, 128], [0, NPT], [1, 32]])
            tA = p.op("V", lambda e: e.tensor_tensor(
                bass.AP(tha, 0, [[WT, 128], [32, NPT], [1, 32]]), ppre, ifre, ALU.mult))
            tB = p.op("V", lambda e: e.tensor_scalar(
                tha[:, NPT * 32: WT], posbf[:, :], invf_c[:, 0:1], None, ALU.mult), after=[pb])
            red = emit_reduce(p, tha, thb, thk, tki, WT, tB)
            sn1 = p.op("S", lambda e: e.activation(tha[:, :], thb[:, :], AF.Sin), after=[red])
            c0 = p.op("V", lambda e: e.tensor_scalar(thk[:, :], thb[:, :], PI / 2.0, None, ALU.add), after=[red])
            c1 = p.op("V", lambda e: e.tensor_scalar(thb[:, :], thk[:, :], PI, None, ALU.is_gt), after=[c0, sn1])
            c2 = p.op("V", lambda e: e.scalar_tensor_tensor(thk[:, :], thb[:, :], -2.0 * PI, thk[:, :], ALU.mult, ALU.add), after=[c1])
            c3 = p.op("V", lambda e: e.tensor_scalar(thk[:, :], thk[:, :], PI, -PI, ALU.min, ALU.max), after=[c2])
            sn2 = p.op("S", lambda e: e.activation(thk[:, :], thk[:, :], AF.Sin), after=[c3])
            f0t = p.op("V", lambda e: e.tensor_scalar(
                bass.AP(sinwA, 0, [[NPT * 64, 128], [64, NPT], [1, 32]]),
                bass.AP(tha, 0, [[WT, 128], [32, NPT], [1, 32]]), -SCALE, None, ALU.mult), after=[sn1])
            f1t = p.op("V", lambda e: e.tensor_scalar(
                bass.AP(sinwA, 32, [[NPT * 64, 128], [64, NPT], [1, 32]]),
                bass.AP(tha, 0, [[WT, 128], [32, NPT], [1, 32]]), SCALE, None, ALU.mult), after=[sn1])
            f2t = p.op("V", lambda e: e.tensor_scalar(
                bass.AP(coswA, 0, [[NPT * 64, 128], [64, NPT], [1, 32]]),
                bass.AP(thk, 0, [[WT, 128], [32, NPT], [1, 32]]), SCALE, None, ALU.mult), after=[sn2])
            f3t = p.op("V", lambda e: e.tensor_scalar(
                bass.AP(coswA, 32, [[NPT * 64, 128], [64, NPT], [1, 32]]),
                bass.AP(thk, 0, [[WT, 128], [32, NPT], [1, 32]]), SCALE, None, ALU.mult), after=[sn2])
            fm0 = p.op("V", lambda e: e.tensor_scalar(sinmM[:, :], tha[:, NPT * 32: WT], sgn[:, 0:1], None, ALU.mult), after=[sn1])
            fm1 = p.op("V", lambda e: e.tensor_scalar(cosmM[:, :], thk[:, NPT * 32: WT], SCALE, None, ALU.mult), after=[sn2])
            run_block(p)

        with ExitStack() as ph:
            psb = lambda n, s, d=F32: ph.enter_context(nc.sbuf_tensor(n, s, d))
            xps = [psb("xps0", [128, 4 * D], BF16)] * 2
            xnp_t = [psb(f"xnp_t{i}", [128, 4 * D], BF16) for i in range(2)]
            xnp_f = psb("xnp_f", [128, KT * 512], BF16)
            kp_t = psb("kp_t", [128, 4 * D], BF16)
            vp_t = psb("vp_t", [128, 4 * D], BF16)
            wk_b = [psb(f"wk_b{i}", [128, KT * 512], BF16) for i in range(2)]
            wv_b = [psb(f"wv_b{i}", [128, KT * 512], BF16) for i in range(2)]
            lnc = [psb(f"lnc{j}", [128, D]) for j in range(2)]
            lcol = [psb(f"lcol{j}", [128, 4]) for j in range(2)]
            rsc1 = [psb(f"rsc1{j}", [128, D], BF16) for j in range(2)]
            rsc2 = [psb(f"rsc2{j}", [128, D], BF16) for j in range(2)]
            psK = [ph.enter_context(nc.psum_tensor(f"psK{i}", [128, 512], F32)) for i in range(2)]
            psV = [ph.enter_context(nc.psum_tensor(f"psV{i}", [128, 512], F32)) for i in range(2)]
            psT = [ph.enter_context(nc.psum_tensor(f"psT{i}", [128, 1024], BF16)) for i in range(2)]
            psS = [ph.enter_context(nc.psum_tensor(f"psS{i}", [128, 512], F32)) for i in range(2)]

            p = Prog(counts)
            wk_tok, wv_tok = [], []
            for hf in range(2):
                wk_tok.append(p.op("G", lambda e, d=wk_b[hf], hf=hf: e.dma_start(
                    out=d[:, :], in_=wkvp_d[hf, :, :]), dma=True))
                wv_tok.append(p.op("G", lambda e, d=wv_b[hf], hf=hf: e.dma_start(
                    out=d[:, :], in_=wkvp_d[2 + hf, :, :]), dma=True))

            ld_tok = [None, None]
            ln_prev = [(None, None, None), (None, None, None)]
            stage_free = [None, None]
            xnpt_free = [None, None]
            xnpf_free = None
            psK_free = [None, None]
            psV_free = [None, None]
            psT_free = [None, None]
            kp_free = None
            vp_free = None
            rprev = [[None, None, None], [None, None, None]]
            s_chain = None
            first_s = [True, True]

            for ptb in range(NPTB):
                par = ptb % 2
                ld_tok[par] = p.op("G", lambda e, b=xps[par], i0=ptb * 512:
                                   e.dma_start(out=r3s(b[:, :], 4), in_=r3d(xpre_d[i0:i0 + 512, :])),
                                   after=[stage_free[0], stage_free[1]], dma=True)
                ln_done = []
                for i in range(4):
                    tok, ln_prev[i % 2] = emit_ln(
                        p,
                        lambda b=xps[par], i=i: b[:, i * D:(i + 1) * D],
                        lambda i=i, par=par: xnp_t[par][:, i * D:(i + 1) * D],
                        lnc[i % 2], lcol[i % 2],
                        src_after=[ld_tok[par]], dst_after=[xnpt_free[par]], prev=ln_prev[i % 2])
                    ln_done.append(tok)
                stage_free[par] = ln_done[-1]

                tr_last = None
                for i in range(4):
                    for k in range(KT):
                        tp_ = (i * KT + k) % 2
                        tm = p.op("T", lambda e, i=i, k=k, tp_=tp_, par=par: e.transpose(
                            psT[tp_][:, 0:128], xnp_t[par][:, i * D + k * 128: i * D + (k + 1) * 128], ident_b[:, :]),
                            after=[ln_done[i], psT_free[tp_], xnpf_free])
                        psT_free[tp_] = p.op("S", lambda e, i=i, k=k, tp_=tp_: e.copy(
                            xnp_f[:, k * 512 + i * 128: k * 512 + (i + 1) * 128], psT[tp_][:, 0:128]), after=[tm])
                        tr_last = psT_free[tp_]
                xnpt_free[par] = tr_last
                xnpf_ready_l = [psT_free[0], psT_free[1]]
                xnpf_ready = tr_last

                kp_ready = {}
                vp_ready = {}
                for i in range(4):
                    for kind in range(2):
                        for hf in range(2):
                            bank = (psK if kind == 0 else psV)[hf]
                            wb = (wk_b if kind == 0 else wv_b)[hf]
                            wt = (wk_tok if kind == 0 else wv_tok)[hf]
                            free = (psK_free if kind == 0 else psV_free)[hf]

                            def mmkv(e, i=i, b=bank, w=wb):
                                last = None
                                for k in range(KT):
                                    last = e.matmul(b[:, :],
                                                    xnp_f[:, k * 512 + i * 128: k * 512 + (i + 1) * 128],
                                                    w[:, k * 512:(k + 1) * 512],
                                                    start=(k == 0), stop=(k == KT - 1))
                                return last
                            mt = p.op("T", mmkv, after=xnpf_ready_l + [wt, free])
                            dst = kp_t if kind == 0 else vp_t
                            eng_ev = "V" if kind == 0 else "S"
                            ev = p.op(eng_ev, lambda e, d=dst, i=i, hf=hf, b=bank, k0=(kind == 0):
                                      (e.tensor_copy if k0 else e.copy)(
                                      d[:, i * D + hf * 512: i * D + hf * 512 + 512], b[:, :]),
                                      after=[mt, kp_free if kind == 0 else vp_free])
                            if kind == 0:
                                psK_free[hf] = ev
                            else:
                                psV_free[hf] = ev
                            if hf == 1:
                                (kp_ready if kind == 0 else vp_ready)[i] = ev
                xnpf_free = mt

                for i in range(4):
                    ti = ptb * 4 + i
                    pj = i % 2
                    s1b, s2b = rsc1[pj], rsc2[pj]
                    cosb = bass.AP(coswA, ti * 64, [[NPT * 64, 128], [0, 16], [1, 64]])
                    sinb = bass.AP(sinwA, ti * 64, [[NPT * 64, 128], [0, 16], [1, 64]])
                    r0 = p.op("V", lambda e, i=i, cb=cosb, s1b=s1b: e.tensor_tensor(
                        s1b[:, :], kp_t[:, i * D:(i + 1) * D], cb, ALU.mult),
                        after=[kp_ready[i], rprev[pj][0]])
                    r1 = p.op("V", lambda e, i=i, sbp=sinb, s2b=s2b: e.tensor_tensor(
                        s2b[:, :], bass.AP(kp_t, i * D + 32, [[4 * D, 128], [64, 16], [-32, 2], [1, 32]]),
                        sbp, ALU.mult), after=[kp_ready[i], rprev[pj][1], rprev[pj][2]])
                    r2 = p.op("V", lambda e, s1b=s1b, s2b=s2b: e.tensor_tensor(s1b[:, :], s1b[:, :], s2b[:, :], ALU.add), after=[r0, r1])
                    r3 = p.op("V", lambda e, s1b=s1b, s2b=s2b: e.tensor_scalar(s2b[:, :], s1b[:, :], 0.0, None, ALU.min), after=[r2])
                    r4 = p.op("S", lambda e, s2b=s2b: e.activation(s2b[:, :], s2b[:, :], AF.Exp), after=[r3])
                    r5 = p.op("V", lambda e, s1b=s1b: e.tensor_scalar(s1b[:, :], s1b[:, :], 0.0, None, ALU.max), after=[r2, r3])
                    r6 = p.op("V", lambda e, s1b=s1b, s2b=s2b: e.tensor_tensor(s1b[:, :], s1b[:, :], s2b[:, :], ALU.add), after=[r5, r4])
                    r7 = p.op("V", lambda e, i=i, ptb=ptb, s1b=s1b: e.tensor_scalar(
                        kp_t[:, i * D:(i + 1) * D], s1b[:, :],
                        prem_s[:, ptb * 4 + i: ptb * 4 + i + 1], None, ALU.mult), after=[r6, kp_free])
                    kp_ready[i] = r7
                    rprev[pj] = [r7, r6, r4]

                for i in range(4):
                    def smm(e, i=i, first=first_s[0], last_acc=(ptb == NPTB - 1 and i == 3)):
                        last = None
                        for h in range(HEADS):
                            bank = psS[h // 8]
                            rb = 64 * (h % 2)
                            cb = ((h % 8) // 2) * 64
                            st = first and (h % 8 in (0, 1))
                            sp = last_acc and (h % 8 in (6, 7))
                            last = e.matmul(bank[rb:rb + 64, cb:cb + 64],
                                            kp_t[:, i * D + h * HD: i * D + (h + 1) * HD],
                                            vp_t[:, i * D + h * HD: i * D + (h + 1) * HD],
                                            start=st, stop=sp, tile_position=(0, rb),
                                            skip_group_check=True)
                            last = e.matmul(psS[h // 8][rb:rb + 64, 256 + (h % 8): 257 + (h % 8)],
                                            kp_t[:, i * D + h * HD: i * D + (h + 1) * HD],
                                            ones_b[:, :],
                                            start=False, stop=False,
                                            tile_position=(0, rb), skip_group_check=True)
                        return last
                    s_chain = p.op("T", smm, after=[kp_ready[i], vp_ready[i], s_chain])
                    first_s = [False, False]
                kp_free = s_chain
                vp_free = s_chain

            evs = []
            for h in range(HEADS):
                rb, scol = stpos(h)
                cb = ((h % 8) // 2) * 64
                evs.append(p.op("V", lambda e, h=h, rb=rb, scol=scol, cb=cb: e.tensor_copy(
                    st_f[rb:rb + 64, scol:scol + 64], psS[h // 8][rb:rb + 64, cb:cb + 64]), after=[s_chain]))
                evs.append(p.op("V", lambda e, h=h, rb=rb, scol=scol: e.tensor_copy(
                    st_f[rb:rb + 64, scol + 64:scol + 65],
                    psS[h // 8][rb:rb + 64, 256 + (h % 8): 257 + (h % 8)]), after=[s_chain]))
            ev2 = p.op("V", lambda e: e.tensor_copy(st_b[:, :], st_f[:, :]), after=evs)
            if dbg:
                p.op("Y", lambda e: e.dma_start(out=dbg_d["d_st"][:, :], in_=st_f[:, :]), after=[ev2], dma=True)
            run_block(p)
        tabstk.close()

        mid = ExitStack()
        qf = mid.enter_context(nc.sbuf_tensor("qf", [128, KT * n_own], BF16))
        kf = mid.enter_context(nc.sbuf_tensor("kf", [128, KT * n_own], BF16))
        vt = mid.enter_context(nc.sbuf_tensor("vt", [128, NT * D], BF16))
        with ExitStack() as ph:
            psb = lambda n, s, d=F32: ph.enter_context(nc.sbuf_tensor(n, s, d))
            xn_t = psb("xn_t", [128, D], BF16)
            xn_f = psb("xn_f", [128, KT * 512], BF16)
            lnc = [psb("lnc2", [128, D])] * 2
            lcol = [psb(f"lcol2{j}", [128, 4]) for j in range(2)]
            ra = [psb(f"ra{j}", [128, 512]) for j in range(2)]
            rb_ = [psb(f"rb{j}", [128, 512]) for j in range(2)]
            wqk = [psb(f"wqk{i}", [128, KT * 128], BF16) for i in range(2)]
            wvb = psb("wvb", [128, KT * 512], BF16)
            psQ = [ph.enter_context(nc.psum_tensor(f"psQ{i}", [128, 512], F32)) for i in range(4)]
            psV2 = [ph.enter_context(nc.psum_tensor(f"psV2{i}", [128, 512], F32)) for i in range(2)]
            psT2 = [ph.enter_context(nc.psum_tensor(f"psT2{i}", [128, 1024], BF16)) for i in range(2)]

            p = Prog(counts)
            xnt_free = []
            dbg_free = []
            ln_prev = [(None, None, None), (None, None, None)]
            xnf_free = None
            psT2_free = [None, None]
            qprev = [[None, None], [None, None]]
            wqk_free = [None] * 2
            wvb_free = None
            psQ_free = [None] * 4
            psV2_free = [None] * 2
            last_write = []

            for tb in range(NTB):
                tr_last = None
                for i in range(4):
                    ti = tb * 4 + i
                    tok, ln_prev[i % 2] = emit_ln(
                        p,
                        lambda ti=ti: x_res[:, ti * D:(ti + 1) * D],
                        lambda: xn_t[:, :],
                        lnc[i % 2], lcol[i % 2], dst_after=xnt_free, prev=ln_prev[i % 2])
                    if dbg:
                        dbg_free.append(p.op("G", lambda e, ti=ti: e.dma_start(
                            out=dbg_d["d_xn1"][ti * 128:(ti + 1) * 128, :],
                            in_=xn_t[:, :]), after=[tok], dma=True))
                    for k in range(KT):
                        tp_ = k % 2
                        tm = p.op("T", lambda e, i=i, k=k, tp_=tp_: e.transpose(
                            psT2[tp_][:, 0:128], xn_t[:, k * 128:(k + 1) * 128], ident_b[:, :]),
                            after=[tok, psT2_free[tp_], xnf_free])
                        psT2_free[tp_] = p.op("V", lambda e, i=i, k=k, tp_=tp_: e.tensor_copy(
                            xn_f[:, k * 512 + i * 128: k * 512 + (i + 1) * 128], psT2[tp_][:, 0:128]), after=[tm])
                        tr_last = psT2_free[tp_]
                    xnt_free = [tr_last, psT2_free[0], psT2_free[1]] + dbg_free
                    dbg_free = []
                xnf_ready_l = [psT2_free[0], psT2_free[1]]
                xnf_ready = tr_last

                for pair, dst in ((0, qf), (1, kf)):
                    for ft in range(KT):
                        bi = 0
                        bq, bp = psQ[(ft % 2) * 2], psQ[(ft % 2) * 2 + 1]
                        i_main = pair * 2 * KT + ft
                        i_perm = i_main + KT
                        t_w0 = p.op("G", lambda e, w=wqk[0], ix=i_main: e.dma_start(
                            out=w[:, :], in_=wqkt_d[ix, :, :]),
                            after=[wqk_free[0]], dma=True)
                        t_w1 = p.op("G", lambda e, w=wqk[1], ix=i_perm: e.dma_start(
                            out=w[:, :], in_=wqkt_d[ix, :, :]),
                            after=[wqk_free[1]], dma=True)

                        bqi = (ft % 2) * 2

                        def mmq(e, w=wqk[0], b=bq):
                            last = None
                            for k in range(KT):
                                last = e.matmul(b[:, :], w[:, k * 128:(k + 1) * 128],
                                                xn_f[:, k * 512:(k + 1) * 512],
                                                start=(k == 0), stop=(k == KT - 1))
                            return last
                        m0 = p.op("T", mmq, after=xnf_ready_l + [t_w0, psQ_free[bqi]])

                        def mmp(e, w=wqk[1], b=bp):
                            last = None
                            for k in range(KT):
                                last = e.matmul(b[:, :], w[:, k * 128:(k + 1) * 128],
                                                xn_f[:, k * 512:(k + 1) * 512],
                                                start=(k == 0), stop=(k == KT - 1))
                            return last
                        m1 = p.op("T", mmp, after=[t_w1, psQ_free[bqi + 1], m0])
                        wqk_free[0] = m0
                        wqk_free[1] = m1
                        pj = ft % 2
                        rab, rbb = ra[pj], rb_[pj]
                        r0 = p.op("V", lambda e, b=bq, tb=tb, rab=rab: e.tensor_tensor(
                            rab[:, :], b[:, :], cosmM[:, tb * 512:(tb + 1) * 512], ALU.mult),
                            after=[m0, qprev[pj][0]])
                        r1 = p.op("V", lambda e, b=bp, tb=tb, rbb=rbb: e.tensor_tensor(
                            rbb[:, :], b[:, :], sinmM[:, tb * 512:(tb + 1) * 512], ALU.mult),
                            after=[m1, qprev[pj][1]])
                        r2 = p.op("V", lambda e, rab=rab, rbb=rbb: e.tensor_tensor(rab[:, :], rab[:, :], rbb[:, :], ALU.add), after=[r0, r1])
                        r3 = p.op("V", lambda e, rab=rab, rbb=rbb: e.tensor_scalar(rbb[:, :], rab[:, :], 0.0, None, ALU.min), after=[r2])
                        r4 = p.op("S", lambda e, rbb=rbb: e.activation(rbb[:, :], rbb[:, :], AF.Exp), after=[r3])
                        r5 = p.op("V", lambda e, rab=rab: e.tensor_scalar(rab[:, :], rab[:, :], 0.0, None, ALU.max), after=[r2, r3])
                        r6 = p.op("V", lambda e, d=dst, ft=ft, tb=tb, rab=rab, rbb=rbb: e.tensor_tensor(
                            d[:, ft * n_own + tb * 512: ft * n_own + (tb + 1) * 512],
                            rab[:, :], rbb[:, :], ALU.add), after=[r5, r4])
                        qprev[pj] = [r6, r4]
                        psQ_free[bqi] = r6
                        psQ_free[bqi + 1] = r6
                        last_write.append(r6)

                for hf in range(2):
                    t_wv = p.op("G", lambda e, hf=hf: e.dma_start(
                        out=wvb[:, :], in_=wkvp_d[2 + hf, :, :]), after=[wvb_free], dma=True)
                    for i in range(4):
                        ti = tb * 4 + i
                        bank = psV2[i % 2]

                        def mmv(e, i=i, b=bank):
                            last = None
                            for k in range(KT):
                                last = e.matmul(b[:, :], xn_f[:, k * 512 + i * 128: k * 512 + (i + 1) * 128],
                                                wvb[:, k * 512:(k + 1) * 512],
                                                start=(k == 0), stop=(k == KT - 1))
                            return last
                        mv = p.op("T", mmv, after=xnf_ready_l + [t_wv, psV2_free[i % 2]])
                        wvb_free = mv
                        ev = p.op("S", lambda e, ti=ti, hf=hf, b=bank: e.copy(
                            vt[:, ti * D + hf * 512: ti * D + hf * 512 + 512], b[:, :]), after=[mv])
                        psV2_free[i % 2] = ev
                        last_write.append(ev)
                xnf_free = wvb_free

            if dbg:
                lw = last_write[-20:]
                p.op("G", lambda e: e.dma_start(out=dbg_d["d_qf"][:, :], in_=qf[:, :]), after=lw, dma=True)
                p.op("G", lambda e: e.dma_start(out=dbg_d["d_kf"][:, :], in_=kf[:, :]), after=lw, dma=True)
                p.op("G", lambda e: e.dma_start(
                    out=r3d(dbg_d["d_vt"][:, :]), in_=r3s(vt[:, :], NT)), after=lw, dma=True)
            run_block(p)

        with ExitStack() as ph:
            psb = lambda n, s, d=F32: ph.enter_context(nc.sbuf_tensor(n, s, d))
            am = [psb(f"am{i}", [128, 128], BF16) for i in range(2)]
            kcs = [psb(f"kcs{i}", [128, 64], BF16) for i in range(2)]
            dcol = [psb(f"dcol{j}", [128, 2]) for j in range(2)]
            otc = psb("otc", [128, D], BF16)
            otf = psb("otf", [128, KT * 128], BF16)
            woutb = psb("woutb", [128, KT * D], BF16)
            pA = [ph.enter_context(nc.psum_tensor(f"pA{i}", [128, 512], F32)) for i in range(2)]
            pO = [ph.enter_context(nc.psum_tensor(f"pO{i}", [128, 512], F32)) for i in range(2)]
            pS2 = ph.enter_context(nc.psum_tensor("pS2", [128, 512], F32))
            pB16 = ph.enter_context(nc.psum_tensor("pB16", [128, 1024], BF16))
            pP = [ph.enter_context(nc.psum_tensor(f"pP{i}", [128, 512], F32)) for i in range(2)]

            p = Prog(counts)
            t_wo = p.op("G", lambda e: e.dma_start(
                out=woutb[:, :], in_=wout_d[:, :]), dma=True)

            am_free = [None, None]
            dcol_free = [None, None]
            kcs_free = [None, None]
            pA_free = [None, None]
            pO_free = [None, None]
            pS2_free = None
            pB16_free = None
            pP_free = [None, None]
            st_rd = {h: None for h in range(HEADS)}
            st_wr = {h: None for h in range(HEADS)}
            otc_wr = [None] * HEADS
            otc_free = None
            otf_free = [None] * KT
            x1_toks = []

            def qsl(buf, h, c):
                return buf[64 * (h % 2):64 * (h % 2) + 64,
                           (h // 2) * n_own + c * 128:(h // 2) * n_own + (c + 1) * 128]

            for c in range(NCH):
                for h in range(HEADS):
                    par = h % 2
                    base, scol = stpos(h)
                    a0 = p.op("T", lambda e, h=h, c=c, par=par: e.matmul(
                        pA[par][:, 0:128], qsl(kf, h, c), qsl(qf, h, c), start=True, stop=True),
                        after=[pA_free[par]])
                    a1 = p.op("V", lambda e, par=par: e.tensor_tensor(
                        am[par][:, :], pA[par][:, 0:128], cmask_b[:, :], ALU.mult), after=[a0, am_free[par]])
                    pA_free[par] = a1
                    k0 = p.op("T", lambda e, h=h, c=c, base=base: e.transpose(
                        pB16[:, 0:64], qsl(kf, h, c), ident_b[base:base + 64, base:base + 64]),
                        after=[pB16_free])
                    k1 = p.op("S", lambda e, par=par: e.copy(kcs[par][:, :], pB16[:, 0:64]),
                              after=[k0, kcs_free[par]])
                    pB16_free = k1

                    def mmo(e, h=h, c=c, par=par, base=base, scol=scol):
                        ob = pO[par]
                        e.matmul(ob[:, 0:64], am[par][:, :], vt[:, c * D + h * HD: c * D + (h + 1) * HD],
                                 start=True, stop=False)
                        e.matmul(ob[:, 0:64], qsl(qf, h, c), st_b[base:base + 64, scol:scol + 64],
                                 start=False, stop=False)
                        e.matmul(ob[:, 64:65], am[par][:, :], ones_b[:, :], start=False, stop=False)
                        return e.matmul(ob[:, 64:65], qsl(qf, h, c),
                                        st_b[base:base + 64, scol + 64:scol + 65], start=False, stop=True)
                    o0 = p.op("T", mmo, after=[a1, pO_free[par], st_wr[h]])
                    st_rd[h] = o0
                    am_free[par] = o0
                    d0 = p.op("V", lambda e, par=par: e.tensor_scalar(
                        dcol[par][:, 0:1], pO[par][:, 64:65], 1e-6, None, ALU.max), after=[o0])
                    d1 = p.op("V", lambda e, par=par: e.reciprocal(dcol[par][:, 1:2], dcol[par][:, 0:1]),
                              after=[d0, dcol_free[par]])
                    d2 = p.op("S", lambda e, h=h, par=par: e.activation(
                        otc[:, h * HD:(h + 1) * HD], pO[par][:, 0:64], AF.Copy, scale=dcol[par][:, 1:2]),
                        after=[d1, otc_free])
                    dcol_free[par] = d2
                    pO_free[par] = d2
                    otc_wr[h] = d2

                    def mms(e, h=h, c=c, base=base, par=par):
                        e.matmul(pS2[base:base + 64, 0:64], kcs[par][:, :],
                                 vt[:, c * D + h * HD: c * D + (h + 1) * HD],
                                 start=True, stop=False, tile_position=(0, base))
                        return e.matmul(pS2[base:base + 64, 64:65], kcs[par][:, :], ones_b[:, :],
                                        start=False, stop=True, tile_position=(0, base))
                    s0 = p.op("T", mms, after=[k1, pS2_free])
                    kcs_free[par] = s0
                    s1 = p.op("V", lambda e, base=base, scol=scol: e.tensor_tensor(
                        st_f[base:base + 64, scol:scol + 65], st_f[base:base + 64, scol:scol + 65],
                        pS2[base:base + 64, 0:65], ALU.add), after=[s0, st_rd[h]])
                    s2 = p.op("S", lambda e, base=base, scol=scol: e.copy(
                        st_b[base:base + 64, scol:scol + 65], st_f[base:base + 64, scol:scol + 65]), after=[s1])
                    pS2_free = s2
                    st_wr[h] = s2

                trs = []
                for k in range(KT):
                    tm = p.op("T", lambda e, k=k: e.transpose(
                        pB16[:, 0:128], otc[:, k * 128:(k + 1) * 128], ident_b[:, :]),
                        after=[pB16_free, otc_wr[2 * k], otc_wr[2 * k + 1], otf_free[k]])
                    pB16_free = p.op("V", lambda e, k=k: e.tensor_copy(
                        otf[:, k * 128:(k + 1) * 128], pB16[:, 0:128]), after=[tm])
                    trs.append(pB16_free)
                otc_free = trs[-1]

                def mmp1(e):
                    last = None
                    for k in range(KT):
                        last = e.matmul(pP[0][:, :], otf[:, k * 128:(k + 1) * 128],
                                        woutb[:, k * D: k * D + 512],
                                        start=(k == 0), stop=(k == KT - 1))
                    return last
                pm0 = p.op("T", mmp1, after=trs + [t_wo, pP_free[0]])

                def mmp2(e):
                    last = None
                    for k in range(KT):
                        last = e.matmul(pP[1][:, :], otf[:, k * 128:(k + 1) * 128],
                                        woutb[:, k * D + 512: (k + 1) * D],
                                        start=(k == 0), stop=(k == KT - 1))
                    return last
                pm1 = p.op("T", mmp2, after=[pm0, pP_free[1]])
                for k in range(KT):
                    otf_free[k] = pm1
                e0 = p.op("V", lambda e, c=c: e.tensor_tensor(
                    x_res[:, c * D: c * D + 512], x_res[:, c * D: c * D + 512], pP[0][:, :], ALU.add), after=[pm0])
                e1 = p.op("V", lambda e, c=c: e.tensor_tensor(
                    x_res[:, c * D + 512: (c + 1) * D], x_res[:, c * D + 512: (c + 1) * D], pP[1][:, :], ALU.add), after=[pm1])
                pP_free[0] = e0
                pP_free[1] = e1
                x1_toks = [e0, e1]

            if dbg:
                p.op("Y", lambda e: e.dma_start(
                    out=r3d(dbg_d["d_x1"][:, :]), in_=r3s(x_res[:, :], NT)), after=x1_toks, dma=True)
            run_block(p)
        mid.close()

        xn2f = sb("xn2f", [128, KT * n_own], BF16)
        with ExitStack() as ph:
            psb = lambda n, s, d=F32: ph.enter_context(nc.sbuf_tensor(n, s, d))
            xn_t = [psb(f"xn_t2{j}", [128, D], BF16) for j in range(2)]
            lnc = [psb(f"lnc3{j}", [128, D]) for j in range(2)]
            lcol = [psb(f"lcol3{j}", [128, 4]) for j in range(2)]
            rwtb = psb("rwtb", [128, KT * 8], BF16)
            lg = psb("lg", [128, 8])
            l2t = psb("l2t", [128, 8])
            m12 = psb("m12", [128, 6])
            pR = ph.enter_context(nc.psum_tensor("pR", [128, 512], F32))
            pT4 = [ph.enter_context(nc.psum_tensor(f"pT4{i}", [128, 1024], BF16)) for i in range(2)]

            p = Prog(counts)
            t_rw = p.op("G", lambda e: e.dma_start(
                out=rwtb[:, :], in_=rwt_d[:, :]), dma=True)
            pT4_free = [None, None]
            pR_free = None
            xnt_free2 = [[], []]
            dbg_free = []
            w8_last = None
            ln_prev = [(None, None, None), (None, None, None)]
            for ti in range(NT):
                pj = ti % 2
                a7, ln_prev[pj] = emit_ln(
                    p,
                    lambda ti=ti: x_res[:, ti * D:(ti + 1) * D],
                    lambda pj=pj: xn_t[pj][:, :],
                    lnc[pj], lcol[pj], dst_after=xnt_free2[pj], prev=ln_prev[pj])
                if dbg:
                    dbg_free.append(p.op("G", lambda e, ti=ti, pj=pj: e.dma_start(
                        out=dbg_d["d_xn2"][ti * 128:(ti + 1) * 128, :],
                        in_=xn_t[pj][:, :]), after=[a7], dma=True))
                trs = []
                for k in range(KT):
                    tp_ = k % 2
                    tm = p.op("T", lambda e, k=k, tp_=tp_, pj=pj: e.transpose(
                        pT4[tp_][:, 0:128], xn_t[pj][:, k * 128:(k + 1) * 128], ident_b[:, :]),
                        after=[a7, pT4_free[tp_]])
                    pT4_free[tp_] = p.op("V", lambda e, k=k, ti=ti, tp_=tp_: e.tensor_copy(
                        xn2f[:, k * n_own + ti * 128: k * n_own + (ti + 1) * 128], pT4[tp_][:, 0:128]), after=[tm])
                    trs.append(pT4_free[tp_])
                xnt_free2[pj] = [trs[-1], trs[-2]] + dbg_free
                dbg_free = []

                def mmr(e, ti=ti):
                    last = None
                    for k in range(KT):
                        last = e.matmul(pR[:, 0:8], xn2f[:, k * n_own + ti * 128: k * n_own + (ti + 1) * 128],
                                        rwtb[:, k * 8:(k + 1) * 8], start=(k == 0), stop=(k == KT - 1))
                    return last
                r0 = p.op("T", mmr, after=trs + [t_rw, pR_free])
                b0 = p.op("V", lambda e: e.tensor_tensor(lg[:, :], pR[:, 0:8], bbc_s[:, :], ALU.add), after=[r0])
                pR_free = b0
                b1 = p.op("V", lambda e: e.reduce_max(m12[:, 0:1], lg[:, :], axis=AX.X), after=[b0])
                b2 = p.op("V", lambda e: e.tensor_scalar(l2t[:, :], lg[:, :], m12[:, 0:1], None, ALU.is_equal), after=[b1])
                b3 = p.op("V", lambda e: e.scalar_tensor_tensor(l2t[:, :], l2t[:, :], -1e30, lg[:, :], ALU.mult, ALU.add), after=[b2])
                b4 = p.op("V", lambda e: e.reduce_max(m12[:, 1:2], l2t[:, :], axis=AX.X), after=[b3])
                b5 = p.op("V", lambda e: e.tensor_tensor(m12[:, 2:3], m12[:, 0:1], m12[:, 1:2], ALU.subtract), after=[b4])
                b6 = p.op("S", lambda e: e.activation(m12[:, 3:4], m12[:, 2:3], AF.Sigmoid), after=[b5])
                b7 = p.op("S", lambda e: e.activation(m12[:, 4:5], m12[:, 2:3], AF.Sigmoid, scale=-1.0), after=[b6])
                b8 = p.op("V", lambda e: e.tensor_scalar(lg[:, :], lg[:, :], m12[:, 0:1], None, ALU.is_equal), after=[b5])
                b9 = p.op("V", lambda e: e.tensor_scalar(l2t[:, :], l2t[:, :], m12[:, 1:2], None, ALU.is_equal), after=[b5])
                ba = p.op("V", lambda e: e.tensor_scalar(lg[:, :], lg[:, :], m12[:, 3:4], None, ALU.mult), after=[b8, b6])
                w8_last = p.op("V", lambda e, ti=ti: e.scalar_tensor_tensor(
                    w8_s[:, ti * 8:(ti + 1) * 8], l2t[:, :], m12[:, 4:5], lg[:, :], ALU.mult, ALU.add),
                    after=[b9, ba, b7])
            if dbg:
                p.op("Y", lambda e: e.dma_start(out=dbg_d["d_w8"][:, :], in_=w8_s[:, :]), after=[w8_last], dma=True)
            run_block(p)

        with ExitStack() as ph:
            psb = lambda n, s, d=F32: ph.enter_context(nc.sbuf_tensor(n, s, d))
            gb = [psb(f"gb{i}", [128, KT * 512], BF16) for i in range(2)]
            ub = [psb(f"ub{i}", [128, KT * 512], BF16) for i in range(2)]
            db_ = [psb(f"db{i}", [128, 4 * D], BF16) for i in range(2)]
            hs = psb("hs", [128, 4 * n_own], BF16)
            sil = [psb(f"sil{i}", [128, 512]) for i in range(2)]
            pG = [ph.enter_context(nc.psum_tensor(f"pG{i}", [128, 512], F32)) for i in range(2)]
            pU = [ph.enter_context(nc.psum_tensor(f"pU{i}", [128, 512], F32)) for i in range(2)]
            pD = [ph.enter_context(nc.psum_tensor(f"pD{i}", [128, 512], F32)) for i in range(2)]

            p = Prog(counts)
            wg_free = [None, None]
            wu_free = [None, None]
            wd_free = [None, None]
            pG_free = [None, None]
            pU_free = [None, None]
            pD_free = [None, None]
            sil_free = [None, None]
            hs_free = None
            out_toks = []

            for ei in range(E):
                for fb in range(NFB):
                    par = (ei * NFB + fb) % 2
                    tg = p.op("G", lambda e, ei=ei, fb=fb, par=par: e.dma_start(
                        out=gb[par][:, :], in_=gw_d[ei * NFB + fb, :, :]),
                        after=[wg_free[par]], dma=True)
                    tu = p.op("G", lambda e, ei=ei, fb=fb, par=par: e.dma_start(
                        out=ub[par][:, :], in_=uw_d[ei * NFB + fb, :, :]),
                        after=[wu_free[par]], dma=True)
                    td = p.op("G", lambda e, ei=ei, fb=fb, par=par: e.dma_start(
                        out=db_[par][:, :], in_=dw_d[ei * NFB + fb, :, :]),
                        after=[wd_free[par]], dma=True)
                    for tb in range(NTB):
                        p1_done = []
                        for f0 in range(4):
                            bpar = f0 % 2

                            def mmg(e, par=par, f0=f0, tb=tb, bpar=bpar):
                                last = None
                                for k in range(KT):
                                    last = e.matmul(pG[bpar][:, :],
                                                    gb[par][:, k * 512 + f0 * 128: k * 512 + (f0 + 1) * 128],
                                                    xn2f[:, k * n_own + tb * 512: k * n_own + (tb + 1) * 512],
                                                    start=(k == 0), stop=(k == KT - 1))
                                return last
                            mg = p.op("T", mmg, after=[tg, pG_free[bpar]])

                            def mmu(e, par=par, f0=f0, tb=tb, bpar=bpar):
                                last = None
                                for k in range(KT):
                                    last = e.matmul(pU[bpar][:, :],
                                                    ub[par][:, k * 512 + f0 * 128: k * 512 + (f0 + 1) * 128],
                                                    xn2f[:, k * n_own + tb * 512: k * n_own + (tb + 1) * 512],
                                                    start=(k == 0), stop=(k == KT - 1))
                                return last
                            mu = p.op("T", mmu, after=[tu, pU_free[bpar], mg])
                            wg_free[par] = mg
                            wu_free[par] = mu
                            sv = p.op("S", lambda e, bpar=bpar: e.activation(
                                sil[bpar][:, :], pG[bpar][:, :], AF.Silu), after=[mg, sil_free[bpar]])
                            pG_free[bpar] = sv
                            hw = p.op("V", lambda e, f0=f0, tb=tb, bpar=bpar: e.tensor_tensor(
                                hs[:, f0 * n_own + tb * 512: f0 * n_own + (tb + 1) * 512],
                                sil[bpar][:, :], pU[bpar][:, :], ALU.mult),
                                after=[sv, mu, hs_free])
                            pU_free[bpar] = hw
                            sil_free[bpar] = hw
                            p1_done.append(hw)
                    last_rd = None
                    for tb in range(NTB):
                        for i in range(4):
                            ti = tb * 4 + i
                            for hf in range(2):
                                bpar = hf

                                def mmd(e, par=par, ti=ti, hf=hf, bpar=bpar):
                                    last = None
                                    for f0 in range(4):
                                        last = e.matmul(pD[bpar][:, :],
                                                        hs[:, f0 * n_own + ti * 128: f0 * n_own + (ti + 1) * 128],
                                                        db_[par][:, f0 * D + hf * 512: f0 * D + (hf + 1) * 512],
                                                        start=(f0 == 0), stop=(f0 == 3))
                                    return last
                                md = p.op("T", mmd, after=p1_done + [td, pD_free[bpar]])
                                wd_free[par] = md
                                last_rd = md
                                ev = p.op("V", lambda e, ti=ti, hf=hf, ei=ei, bpar=bpar: e.scalar_tensor_tensor(
                                    x_res[:, ti * D + hf * 512: ti * D + (hf + 1) * 512],
                                    pD[bpar][:, :], w8_s[:, ti * 8 + ei: ti * 8 + ei + 1],
                                    x_res[:, ti * D + hf * 512: ti * D + (hf + 1) * 512],
                                    ALU.mult, ALU.add), after=[md])
                                pD_free[bpar] = ev
                                out_toks.append(ev)
                    hs_free = last_rd
            run_block(p)

        p = Prog(counts)
        st_tok = p.op("Y", lambda e: e.dma_start(
            out=r3d(out_d[:, :]), in_=r3s(x_res[:, :], NT)),
            after=out_toks[-2 * NT:], dma=True)
        p.op("Y", lambda e: e.nop(), after=[st_tok])
        run_block(p)

    return nc



def _prep_host(L0_attn_x, L0_attn_w_qkv, L0_attn_w_out, L0_ffn_router_w,
               L0_ffn_router_b, L0_ffn_gate_w, L0_ffn_up_w, L0_ffn_down_w,
               n_own, n_pre, f_ffn):
    x = np.ascontiguousarray(np.asarray(L0_attn_x, dtype=np.float32).reshape(-1, D))
    wqkv = np.asarray(L0_attn_w_qkv, dtype=np.float32)
    rw = np.asarray(L0_ffn_router_w, dtype=np.float32)
    rb = np.asarray(L0_ffn_router_b, dtype=np.float32)
    gw = np.asarray(L0_ffn_gate_w, dtype=np.float32)
    uw = np.asarray(L0_ffn_up_w, dtype=np.float32)
    dw = np.asarray(L0_ffn_down_w, dtype=np.float32)

    ntok = x.shape[0]
    assert ntok == NC * n_own

    KTn = D // 128
    NFB = f_ffn // 512

    def pshuf(mat, wtile):
        Wc = mat.shape[1]
        a = mat.reshape(KTn, 128, Wc // wtile, wtile)
        return np.ascontiguousarray(a.transpose(2, 1, 0, 3).reshape(Wc // wtile, 128, KTn * wtile))

    perm = np.arange(D)
    perm = (perm // HD) * HD + ((perm % HD) + HD // 2) % HD
    wq = wqkv[:, 0:D]
    wk = wqkv[:, D:2 * D]
    wv = wqkv[:, 2 * D:3 * D]
    wqkt = np.ascontiguousarray(np.concatenate(
        [pshuf(wq, 128), pshuf(wq[:, perm], 128),
         pshuf(wk, 128), pshuf(wk[:, perm], 128)], axis=0))
    wkvp = np.ascontiguousarray(np.concatenate([pshuf(wk, 512), pshuf(wv, 512)], axis=0))
    wouts = pshuf(np.ascontiguousarray(np.asarray(L0_attn_w_out, dtype=np.float32)), D)[0]
    rwt = pshuf(np.ascontiguousarray(rw.T), 8)[0]
    bbc = np.ascontiguousarray(np.broadcast_to(rb[None, :], (128, 8)).astype(np.float32))
    gwt = np.concatenate([pshuf(np.ascontiguousarray(gw[e].T), 512) for e in range(8)], axis=0)
    uwt = np.concatenate([pshuf(np.ascontiguousarray(uw[e].T), 512) for e in range(8)], axis=0)
    def pshuf_d(mat):
        a = mat.reshape(NFB, 4, 128, D)
        return np.ascontiguousarray(a.transpose(0, 2, 1, 3).reshape(NFB, 128, 4 * D))
    dwt = np.concatenate([pshuf_d(np.ascontiguousarray(dw[e].T)) for e in range(8)], axis=0)

    NPT = n_pre // 128
    r = np.arange(128)
    jcol = (r % 32).astype(np.float32).reshape(128, 1)
    j64 = np.ascontiguousarray(
        np.broadcast_to(np.arange(32, dtype=np.float32)[None, :], (128, 32)))
    sgn = np.where((r % 64) < 32, -np.float32(SCALE), np.float32(SCALE)).astype(np.float32).reshape(128, 1)
    posp = (np.arange(NPT, dtype=np.float32)[None, :] * 128 + r[:, None]).astype(np.float32)
    ident = np.eye(128, dtype=np.float32)
    cmask = np.triu(np.ones((128, 128), dtype=np.float32))
    ccol = np.zeros((128, 2), np.float32)
    ccol[:, 0] = EPS
    ccol[:, 1] = 1.0

    xpre = np.zeros((n_pre, D), dtype=np.float32)
    npre_eff = min(n_pre, (NC - 1) * n_own)
    xpre[:npre_eff] = x[:npre_eff]

    in_maps = []
    for c in range(NC):
        pos0 = c * n_own
        posb = np.ascontiguousarray(np.broadcast_to(
            (pos0 + np.arange(n_own, dtype=np.float32))[None, :], (128, n_own)))
        prem = np.zeros((128, NPT), np.float32)
        npre_c = c * n_own
        full_tiles = npre_c // 128
        prem[:, :full_tiles] = 1.0
        rem = npre_c - full_tiles * 128
        if rem > 0:
            prem[:rem, full_tiles] = 1.0
        in_maps.append(dict(
            x=np.ascontiguousarray(x[c * n_own:(c + 1) * n_own]),
            xpre=xpre, wqkt=wqkt, wkvp=wkvp, wout=wouts, rwt=rwt, bbc=bbc,
            gw=gwt, uw=uwt, dw=dwt,
            posb=posb, posp=posp, jcol=jcol, j64=j64, sgn=sgn,
            prem=prem, ident=ident, cmask=cmask, ccol=ccol,
        ))
    return in_maps


def kernel(L0_attn_x, L0_attn_w_qkv, L0_attn_w_out, L0_ffn_router_w,
           L0_ffn_router_b, L0_ffn_gate_w, L0_ffn_up_w, L0_ffn_down_w):
    n_own = 2048
    n_pre = 14336
    f_ffn = 4096
    in_maps = _prep_host(L0_attn_x, L0_attn_w_qkv, L0_attn_w_out,
                         L0_ffn_router_w, L0_ffn_router_b, L0_ffn_gate_w,
                         L0_ffn_up_w, L0_ffn_down_w, n_own, n_pre, f_ffn)
    nc = build(n_own, n_pre, f_ffn, dbg=False)
    res = run_bass_kernel_spmd(nc, in_maps, core_ids=list(range(NC)))
    out = np.concatenate([np.asarray(r["out"]) for r in res.results], axis=0)
    return out.reshape(1, NC * n_own, D)
